# version 36 (final)
# baseline (speedup 1.0000x reference)
import numpy as np
import ml_dtypes
from contextlib import ExitStack

import concourse.bass as bass
import concourse.mybir as mybir
from concourse.bass_utils import run_bass_kernel_spmd

F32 = mybir.dt.float32
BF16 = mybir.dt.bfloat16
I32 = mybir.dt.int32
AF = mybir.ActivationFunctionType
ALU = mybir.AluOpType
AX = mybir.AxisListType
BF = ml_dtypes.bfloat16

E = 1024
KCH = 8
H = 16
HD = 64
HID = 3072
MCH = 24
TTOK = 256
NT = 16
GC = 3
NGRP = MCH // GC
S127 = float(np.float32(1.0 / 127.0))
S127SQ = float(np.float32(1.0 / 127.0) * np.float32(1.0 / 127.0))
QSCALE = 0.125
LNEPS = 1e-5

ENGINES = ("tensor", "vector", "scalar", "sync", "gpsimd")
SEM_PROD = {
    "pe": "tensor", "ve": "vector", "sc": "scalar",
    "sgq0": "gpsimd", "sgq1": "gpsimd", "sgq2": "gpsimd", "sgq3": "gpsimd",
    "sgo0": "gpsimd", "sgo1": "gpsimd",
    "sgg0": "gpsimd", "sgg1": "gpsimd", "sgg2": "gpsimd", "sgg3": "gpsimd",
    "sgu0": "gpsimd", "sgu1": "gpsimd", "sgu2": "gpsimd", "sgu3": "gpsimd",
    "sgd0": "gpsimd", "sgd1": "gpsimd", "sgd2": "gpsimd", "sgd3": "gpsimd",
    "sgx": "gpsimd",
    "sp1": "gpsimd", "sp2": "sync", "sp3": "sync", "sp4": "gpsimd",
    "sx0": "gpsimd", "sx1": "gpsimd",
    "spt2": "sync", "spt3": "sync", "sps2": "gpsimd", "sps3": "gpsimd",
    "swp0": "sync", "swp1": "sync",
    "svp0": "sync", "svp1": "sync",
    "sd0": "scalar", "sd1": "scalar", "sd2": "scalar", "sd3": "scalar",
    "sy0": "gpsimd", "sy1": "gpsimd",
    "sb0": "gpsimd", "sb1": "gpsimd", "sb2": "gpsimd", "sb3": "gpsimd",
    "sb4": "gpsimd", "sb5": "gpsimd", "sb6": "gpsimd", "sb7": "gpsimd",
    "sb8": "gpsimd", "sb9": "gpsimd", "sb10": "gpsimd", "sb11": "gpsimd",
    "sb12": "gpsimd", "sb13": "gpsimd",
    "spt0": "sync", "spt1": "sync",
    "sps0": "gpsimd", "sps1": "gpsimd",
}


class Prog:

    def __init__(self, live, eng, sems):
        self.live = live
        self.eng = eng
        self.sems = sems
        self.c = {k: 0 for k in SEM_PROD}
        self._waited = {k: 0 for k in SEM_PROD}
        self.guards = {}

    def do(self, engname, emit=None, inc=None, amt=1, waits=()):
        if engname == self.live:
            eng = self.eng[engname]
            for (s, v) in waits:
                force = s.endswith("!")
                if force:
                    s = s[:-1]
                if v <= 0 or (SEM_PROD[s] == engname and not force):
                    continue
                if v <= self._waited[s]:
                    continue
                eng.wait_ge(self.sems[s], v)
                self._waited[s] = v
            inst = emit(eng)
            if inc is not None:
                inst.then_inc(self.sems[inc], amt)
        if inc is not None:
            self.c[inc] += amt

    def acq(self, name):
        g = self.guards.pop(name, [])
        return g

    def rel(self, name, sem):
        self.guards.setdefault(name, []).append((sem, self.c[sem]))


STAGE = 99
EARLY_A = 0
DBG_T = 0
DBG_S = 0


def build_nc(n_tiles=NT, dbg=False):
    nc = bass.Bass("TRN2", target_bir_lowering=False, detect_race_conditions=False)
    T4 = n_tiles * TTOK

    x_in = nc.dram_tensor("x", [T4, E], F32, kind="ExternalInput").ap()
    wqkv = nc.dram_tensor("wqkv", [3 * E, E], F32, kind="ExternalInput").ap()
    wout = nc.dram_tensor("wout", [E, E], F32, kind="ExternalInput").ap()
    wg_i = nc.dram_tensor("wg", [HID, E], I32, kind="ExternalInput").ap()
    wu_i = nc.dram_tensor("wu", [HID, E], I32, kind="ExternalInput").ap()
    wd_i = nc.dram_tensor("wd", [E, HID], I32, kind="ExternalInput").ap()
    bqkvT_in = nc.dram_tensor("bqkvT", [128, 3 * KCH], F32, kind="ExternalInput").ap()
    bout_in = nc.dram_tensor("bout", [1, E], F32, kind="ExternalInput").ap()
    bv_in = nc.dram_tensor("bv", [1, E], F32, kind="ExternalInput").ap()
    ident_in = nc.dram_tensor("ident", [128, 128], BF16, kind="ExternalInput").ap()
    ones_in = nc.dram_tensor("ones", [1, 128], BF16, kind="ExternalInput").ap()
    mask_in = nc.dram_tensor("mask", [128, 128], F32, kind="ExternalInput").ap()
    augc_in = nc.dram_tensor("augc", [128, 256], BF16, kind="ExternalInput").ap()

    y_out = nc.dram_tensor("y", [T4, E], F32, kind="ExternalOutput").ap()

    dbg_t = {}
    if dbg:
        for nm, shape, dt_ in (("d_xT", [128, 2048], BF16), ("d_qkT", [128, 4096], BF16),
                               ("d_vtok", [128, 2048], BF16), ("d_x1T", [128, 4096], BF16),
                               ("d_wgu", [128, 4 * KCH * GC * 128], BF16), ("d_aoT", [128, 2048], BF16),
                               ("d_x1a", [128, 2048], F32), ("d_x1", [128, 2048], F32),
                               ("d_hT", [128, 12288], BF16), ("d_wdb", [128, 2048], BF16),
                               ("d_t2", [128, 2048], F32), ("d_expb", [128, 1024], BF16),
                               ("d_attnT", [128, 1024], BF16), ("d_diag", [128, 1024], BF16),
                               ("d_stats", [128, 64], F32)):
            dbg_t[nm] = nc.dram_tensor(nm, shape, dt_, kind="ExternalOutput").ap()

    wout_nat = nc.dram_tensor("wout_nat", [E, E], BF16, kind="Internal").ap()
    wg_nat = nc.dram_tensor("wg_nat", [HID, E], BF16, kind="Internal").ap()
    wu_nat = nc.dram_tensor("wu_nat", [HID, E], BF16, kind="Internal").ap()
    wd_nat = nc.dram_tensor("wd_nat", [E, HID], BF16, kind="Internal").ap()
    wgT_s = nc.dram_tensor("wgT_s", [E, HID], BF16, kind="Internal").ap()
    wuT_s = nc.dram_tensor("wuT_s", [E, HID], BF16, kind="Internal").ap()
    wdT_s = nc.dram_tensor("wdT_s", [HID, E], BF16, kind="Internal").ap()

    with ExitStack() as ctx:
        en = ctx.enter_context
        RWqkv = en(nc.sbuf_tensor("RWqkv", [128, KCH * 3 * E], BF16))
        RWout = en(nc.sbuf_tensor("RWout", [128, KCH * E], BF16))
        wgu = en(nc.sbuf_tensor("wgu", [128, 2 * 2 * KCH * GC * 128], BF16))
        wdb = en(nc.sbuf_tensor("wdb", [128, 4 * E], BF16))
        xf = en(nc.sbuf_tensor("xf", [128, 2 * 2048], F32))
        xb = en(nc.sbuf_tensor("xb", [128, 2048], BF16))
        xT = en(nc.sbuf_tensor("xT", [128, 2048], BF16))
        qkT = en(nc.sbuf_tensor("qkT", [128, 4096], BF16))
        vtok = en(nc.sbuf_tensor("vtok", [128, 2048], BF16))
        expb = en(nc.sbuf_tensor("expb", [128, 2048], BF16))
        diag = en(nc.sbuf_tensor("diag", [128, 2048], BF16))
        attnT = en(nc.sbuf_tensor("attnT", [128, 2048], BF16))
        aoT = en(nc.sbuf_tensor("aoT", [128, 2048], BF16))
        x1 = en(nc.sbuf_tensor("x1", [128, 4096], F32))
        x1T = en(nc.sbuf_tensor("x1T", [128, 4096], BF16))
        hT = en(nc.sbuf_tensor("hT", [128, MCH * 512], BF16))
        silu2 = en(nc.sbuf_tensor("silu2", [128, 1024], BF16))
        stats = en(nc.sbuf_tensor("stats", [128, 80], F32))
        bqkvT = en(nc.sbuf_tensor("bqkvT_s", [128, 3 * KCH], F32))
        ident = en(nc.sbuf_tensor("ident_s", [128, 128], BF16))
        augcS = en(nc.sbuf_tensor("augcS", [128, 256], BF16))
        onesb = en(nc.sbuf_tensor("onesb", [1, 128], BF16))
        stgb = en(nc.sbuf_tensor("stgb", [1, E], F32))
        brow = en(nc.sbuf_tensor("brow", [1, 2 * E], BF16))
        ps01 = en(nc.psum_tensor("ps01", [128, 1024], F32))
        ps23 = en(nc.psum_tensor("ps23", [128, 1024], F32))
        ps45 = en(nc.psum_tensor("ps45", [128, 1024], F32))
        ps67 = en(nc.psum_tensor("ps67", [128, 1024], F32))
        PS = {"01": ps01, "23": ps23, "45": ps45, "67": ps67}

        sems = {k: en(nc.semaphore(k)) for k in SEM_PROD}
        global LAST_SEMS
        LAST_SEMS = {k: s.num for k, s in sems.items()}
        block = en(nc.Block())

        NEGMAX = 0
        SUMS = 4
        RCPC = 52
        RCP = 36
        LNS = 36
        S2c, Q2c, M2c, MS2c, V2c, R2c, EPSC = 68, 72, 36, 40, 44, 48, 76

        def body(P: Prog):
            mil = {}

            def xsrc(t):
                return x_in[t * TTOK:(t + 1) * TTOK, :].rearrange("(st pp) j -> pp st j", pp=128)

            for dst, src in ((ident[:], ident_in), (onesb[:], ones_in),
                             (bqkvT[:], bqkvT_in), (stgb[:], bout_in), (augcS[:], augc_in)):
                P.do("gpsimd", lambda s, d=dst, sr=src: s.dma_start(out=d, in_=sr), inc="sp1", amt=16)
            mil["consts"] = P.c["sp1"]
            for t in range(min(2, n_tiles)):
                P.do("gpsimd", lambda s, t=t: s.dma_start(
                    out=xf[:, (t % 2) * 2048:(t % 2) * 2048 + 2048].rearrange("p (st j) -> p st j", st=2),
                    in_=xsrc(t)), inc="sx" + str(t % 2), amt=16)
                mil["xload", t] = P.c["sx" + str(t % 2)]
            P.do("scalar", lambda s: s.activation(out=brow[0:1, 0:E], in_=stgb[0:1, :], func=AF.Copy),
                 inc="sc", waits=[("sp1", mil["consts"])])
            mil["bcast1"] = P.c["sc"]
            P.do("gpsimd", lambda s: s.dma_start(out=stgb[:], in_=bv_in), inc="sp4", amt=16,
                 waits=[("sc", mil["bcast1"])])
            mil["bvload"] = P.c["sp4"]
            P.do("scalar", lambda s: s.activation(out=brow[0:1, E:2 * E], in_=stgb[0:1, :], func=AF.Copy),
                 inc="sc", waits=[("sp4", mil["bvload"])])
            mil["bcast"] = P.c["sc"]
            P.do("vector", lambda v: v.tensor_scalar_mul(bqkvT[:, 0:KCH], bqkvT[:, 0:KCH], QSCALE),
                 inc="ve", waits=[("sp1", mil["consts"])])
            mil["bqscale"] = P.c["ve"]
            P.do("vector", lambda v: v.memset(stats[:, EPSC:EPSC + 1], LNEPS), inc="ve")
            mil["epsc"] = P.c["ve"]
            for cc in range(2):
                P.do("gpsimd", lambda g, cc=cc: g.dma_start(
                    out=wout_nat[:, cc * 512:(cc + 1) * 512],
                    in_=wout[:, cc * 512:(cc + 1) * 512]), inc="sgo" + str(cc), amt=16)
            def emit_qkv_chunk(c):
                sl = c % 4
                P.do("gpsimd", lambda g, c=c, sl=sl: g.dma_start(
                    out=x1[:, sl * 1024:(sl + 1) * 1024],
                    in_=wqkv[c * 128:(c + 1) * 128, :]),
                    inc="sgq" + str(sl), amt=16,
                    waits=[("ve", mil.get(("qkvcast", c - 4), 0))])
                mil["qkvld", c] = P.c["sgq" + str(sl)]
                P.do("vector", lambda v, c=c, sl=sl: v.tensor_copy(
                    out=hT[:, sl * 1024:(sl + 1) * 1024],
                    in_=x1[:, sl * 1024:(sl + 1) * 1024]),
                    inc="ve",
                    waits=[("sgq" + str(sl), mil["qkvld", c]),
                           ("pe", mil.get(("qkvmm", c - 4), 0))])
                mil["qkvcast", c] = P.c["ve"]
                pname = "ps23" if c % 2 == 0 else "ps45"
                pair = ps23 if c % 2 == 0 else ps45
                w = P.acq(pname) + [("ve", mil["qkvcast", c]), ("sp1", mil["consts"])]
                for k in range(KCH):
                    P.do("tensor", lambda tn, k=k, sl=sl, pair=pair: tn.matmul(
                        pair[:, k * 128:(k + 1) * 128],
                        hT[:, sl * 1024 + k * 128: sl * 1024 + (k + 1) * 128],
                        ident[:], start=True, stop=True),
                        inc="pe" if k == KCH - 1 else None,
                        waits=w if k == 0 else ())
                mil["qkvmm", c] = P.c["pe"]

            def emit_qkv_evac(c):
                pname = "ps23" if c % 2 == 0 else "ps45"
                pair = ps23 if c % 2 == 0 else ps45
                P.do("vector", lambda v, c=c, pair=pair: v.tensor_copy(
                    out=RWqkv[:].rearrange("p (k j) -> p k j", j=3 * E)[:, :, c * 128:(c + 1) * 128],
                    in_=pair[:].rearrange("p (k j) -> p k j", j=128)),
                    inc="ve", waits=[("pe", mil["qkvmm", c])])
                mil["qkvevac", c] = P.c["ve"]
                P.rel(pname, "ve")

            for c in range(3 * KCH):
                if c >= 2:
                    emit_qkv_evac(c - 2)
                emit_qkv_chunk(c)
            emit_qkv_evac(3 * KCH - 2)
            emit_qkv_evac(3 * KCH - 1)
            mil["rqkv"] = P.c["ve"]

            for k in range(KCH):
                P.do("sync", lambda s, k=k: s.dma_start(
                    out=RWout[:, k * E:(k + 1) * E],
                    in_=wout_nat[:, k * 128:(k + 1) * 128], transpose=True),
                    inc="sp3", amt=16, waits=[("sgo" + str(k // 4), 16)])
            mil["rwout"] = P.c["sp3"]

            cast_chunks = [
                ("sgg0", lambda g: g.dma_start(out=wg_nat, in_=wg_i)),
                ("sgu0", lambda g: g.dma_start(out=wu_nat, in_=wu_i)),
            ]
            wd_casts = [("sgd0", lambda g: g.dma_start(out=wd_nat, in_=wd_i))]

            pjobs = []
            for nat, dstT in ((wg_nat, wgT_s), (wu_nat, wuT_s)):
                for k in range(KCH):
                    pjobs.append((nat, dstT, k))

            def job_chunkidx(j):
                nat, dstT, k = pjobs[j]
                return 0 if dstT is wgT_s else 1

            def emit_ptrans(j):
                nat, dstT, k = pjobs[j]
                par = j % 4
                stg = wgu[:, par * HID: (par + 1) * HID]
                castsem = ("sgg0" if dstT is wgT_s else "sgu0")
                P.do("sync", lambda q, nat=nat, k=k, stg=stg: q.dma_start(
                    out=stg, in_=nat[:, k * 128:(k + 1) * 128], transpose=True),
                    inc="spt" + str(par), amt=16,
                    waits=[(castsem, 16), ("sps" + str(par), mil.get(("pstore", j - 4), 0))])
                mil["ptrans", j] = P.c["spt" + str(par)]

            def emit_pstore(j):
                nat, dstT, k = pjobs[j]
                par = j % 4
                stg = wgu[:, par * HID: (par + 1) * HID]
                P.do("gpsimd", lambda q, dstT=dstT, k=k, stg=stg: q.dma_start(
                    out=dstT[k * 128:(k + 1) * 128, 0:HID], in_=stg),
                    inc="sps" + str(par), amt=16,
                    waits=[("spt" + str(par), mil["ptrans", j])])
                mil["pstore", j] = P.c["sps" + str(par)]

            def wdb_slot(k):
                base = ((k // 2) % 2) * 2 * E + (k % 2) * E
                return wdb[:, base:base + E]

            def emit_wdt(k, extra=()):
                par = str(k % 2)
                w = [("sgd0", 16)] + list(extra)
                if k >= 2:
                    w.append(("spt" + par + "!", mil["wdt", k - 2]))
                if k >= 4:
                    w.append(("sps" + par, mil["wdst", k - 4]))
                P.do("sync", lambda q, k=k: q.dma_start(
                    out=wdb_slot(k), in_=wd_nat[:, k * 128:(k + 1) * 128], transpose=True),
                    inc="spt" + par, amt=16, waits=w)
                mil["wdt", k] = P.c["spt" + par]

            def emit_wdst(k):
                par = str(k % 2)
                w = [("spt" + par, mil["wdt", k])]
                if k >= 2:
                    w.append(("sps" + par + "!", mil["wdst", k - 2]))
                P.do("gpsimd", lambda q, k=k: q.dma_start(
                    out=wdT_s[k * 128:(k + 1) * 128, 0:E], in_=wdb_slot(k)),
                    inc="sps" + par, amt=16, waits=w)
                mil["wdst", k] = P.c["sps" + par]

            def emit_prep_casts():
                for sem, fn in cast_chunks:
                    P.do("gpsimd", fn, inc=sem, amt=16)

            def emit_prep_casts2():
                for sem, fn in wd_casts:
                    P.do("gpsimd", fn, inc=sem, amt=16)
                for j0 in range(4):
                    emit_ptrans(j0)
                for j in range(len(pjobs)):
                    emit_pstore(j)
                    if j + 4 < len(pjobs):
                        emit_ptrans(j + 4)
                for q_ in range(4):
                    mil["gu_stores" + str(q_)] = P.c["sps" + str(q_)]
                emit_wgu_set(0, 0)
                emit_wgu_set(0, 1)
                for k in range(4):
                    emit_wdt(k)
                for k in range(4):
                    emit_wdst(k)


            def front_x(t, wait_scores=0):
                P.do("vector", lambda v, t=t: v.tensor_copy(
                    out=xb[:], in_=xf[:, (t % 2) * 2048:(t % 2) * 2048 + 2048]),
                    inc="ve",
                    waits=[("sx" + str(t % 2), mil["xload", t]),
                           ("pe", mil.get(("xt_done", t - 1), 0))])
                mil["xbcopy", t] = P.c["ve"]
                for st in range(2):
                    w = P.acq("ps01_0") + P.acq("ps01_1") + [("ve", mil["xbcopy", t]), ("sp1", mil["consts"])]
                    if st == 0 and wait_scores:
                        w.append(("pe", wait_scores))
                    for c in range(KCH):
                        P.do("tensor", lambda tn, st=st, c=c: tn.matmul(
                            ps01[:, c * 128:(c + 1) * 128],
                            xb[:, st * 1024 + c * 128: st * 1024 + (c + 1) * 128],
                            ident[:], start=True, stop=True),
                            inc="pe" if c == KCH - 1 else None,
                            waits=w if c == 0 else ())
                    mil["xt_mm", t, st] = P.c["pe"]
                    P.do("vector", lambda v, st=st: v.tensor_copy(
                        out=xT[:].rearrange("p (k j) -> p k j", j=256)[:, :, st * 128:st * 128 + 128],
                        in_=ps01[:].rearrange("p (k j) -> p k j", j=128)),
                        inc="ve", waits=[("pe", mil["xt_mm", t, st]),
                                         ("sb0", mil.get("dump_xT", 0))])
                    mil["xt_evac", t, st] = P.c["ve"]
                    P.rel("ps01_0", "ve")
                    P.rel("ps01_1", "ve")
                mil["xt_done", t] = P.c["pe"]


            def front_q(t):
                for c in range(16):
                    bank = c % 2
                    w = [("ve", mil["xt_evac", t, 1]), ("ve", mil["rqkv"]),
                         ("sc", mil.get(("qk_evac", t, c - 2), 0))]
                    if c < 2:
                        w += P.acq("ps01_" + str(bank))
                    for k in range(KCH):
                        P.do("tensor", lambda tn, c=c, k=k, bank=bank: tn.matmul(
                            ps01[:, bank * 512:bank * 512 + 256],
                            RWqkv[:, k * 3 * E + c * 128: k * 3 * E + (c + 1) * 128],
                            xT[:, k * 256:(k + 1) * 256],
                            start=(k == 0), stop=(k == KCH - 1)),
                            inc="pe" if k == KCH - 1 else None,
                            waits=w if k == 0 else ())
                    mil["qk_mm", t, c] = P.c["pe"]
                    P.do("scalar", lambda sC, c=c, bank=bank: sC.activation(
                        out=qkT[:, c * 256:(c + 1) * 256],
                        in_=ps01[:, bank * 512:bank * 512 + 256],
                        func=AF.Identity, bias=bqkvT[:, c:c + 1],
                        scale=QSCALE if c < KCH else 1.0),
                        inc="sc",
                        waits=[("pe", mil["qk_mm", t, c]), ("ve", mil["bqscale"]),
                               ("sb1", mil.get("dump_qkT", 0))])
                    mil["qk_evac", t, c] = P.c["sc"]
                mil["qk_done", t] = P.c["sc"]
                P.rel("ps01_0", "sc")
                P.rel("ps01_1", "sc")

            def tile_front(t):
                s, p = t // 2, t % 2
                x1off = p * 2048

                if p == 1 and s > 0:
                    for g in (0, 1):
                        emit_wgu_set(s, g)
                    for k in range(4):
                        emit_wd(s, k)
                for st in range(2):
                    w = P.acq("ps23") + [("sc", mil["bcast"]), ("ve", mil["xt_evac", t, 1])]
                    for half in range(2):
                        for k in range(KCH):
                            P.do("tensor", lambda tn, st=st, half=half, k=k: tn.matmul(
                                ps23[:, half * 512:(half + 1) * 512],
                                xT[:, k * 256 + st * 128: k * 256 + st * 128 + 128],
                                RWqkv[:, k * 3 * E + 2 * E + half * 512: k * 3 * E + 2 * E + (half + 1) * 512],
                                start=(k == 0), stop=False),
                                waits=w if (half == 0 and k == 0) else ())
                        P.do("tensor", lambda tn, half=half: tn.matmul(
                            ps23[:, half * 512:(half + 1) * 512],
                            onesb[0:1, :], brow[0:1, E + half * 512:E + (half + 1) * 512],
                            start=False, stop=True),
                            inc="pe" if half == 1 else None)
                    mil["v_mm", t, st] = P.c["pe"]
                    P.do("vector", lambda v, st=st: v.tensor_copy(
                        out=vtok[:, st * 1024:(st + 1) * 1024], in_=ps23[:]),
                        inc="ve", waits=[("pe", mil["v_mm", t, st]),
                                         ("sb2", mil.get("dump_vtok", 0))])
                    mil["v_evac", t, st] = P.c["ve"]
                    P.rel("ps23", "ve")

                if dbg and t == DBG_T:
                    P.do("gpsimd", lambda q: q.dma_start(out=dbg_t["d_xT"], in_=xT[:]),
                         inc="sb0", amt=16, waits=[("ve", mil["xt_evac", DBG_T, 1])])
                    mil["dump_xT"] = 16
                    P.do("gpsimd", lambda q: q.dma_start(out=dbg_t["d_qkT"], in_=qkT[:]),
                         inc="sb1", amt=16, waits=[("sc", mil["qk_done", DBG_T])])
                    mil["dump_qkT"] = 16
                    P.do("gpsimd", lambda q: q.dma_start(out=dbg_t["d_vtok"], in_=vtok[:]),
                         inc="sb2", amt=16, waits=[("ve", mil["v_evac", DBG_T, 1])])
                    mil["dump_vtok"] = 16

                if STAGE <= 3:
                    return
                def attn_A(st, hh):
                    for hh in (hh,):
                        col = st * 2 + hh
                        pst, pn = (ps45, "ps45") if hh == 0 else (ps67, "ps67")
                        eo = hh * 1024
                        w = P.acq(pn) + [("sc", mil["qk_done", t]), ("sp1", mil["consts"])]
                        for i8, h8 in enumerate((0, 2, 4, 6, 1, 3, 5, 7)):
                            h = hh * 8 + h8
                            slot = (h8 % 2) * 4 + h8 // 2
                            qpo = (h % 2) * 64
                            qc, kc = h // 2, KCH + h // 2
                            P.do("tensor", lambda tn, st=st, slot=slot, qpo=qpo, qc=qc, kc=kc, pst=pst: tn.matmul(
                                pst[:, slot * 128:(slot + 1) * 128],
                                qkT[qpo:qpo + 64, qc * 256 + st * 128: qc * 256 + st * 128 + 128],
                                qkT[qpo:qpo + 64, kc * 256 + st * 128: kc * 256 + st * 128 + 128],
                                start=True, stop=False),
                                waits=w if i8 == 0 else ())
                            P.do("tensor", lambda tn, slot=slot, qpo=qpo, pst=pst: tn.matmul(
                                pst[:, slot * 128:(slot + 1) * 128],
                                augcS[qpo:qpo + 4, 0:128],
                                augcS[qpo:qpo + 4, 128:256],
                                start=False, stop=True),
                                inc="pe" if i8 == 7 else None)
                        mil["scores", t, st, hh] = P.c["pe"]
                        P.do("scalar", lambda sC, pst=pst, eo=eo: sC.activation(
                            out=expb[:, eo:eo + 1024], in_=pst[:], func=AF.Exp),
                            inc="sc",
                            waits=[("pe", mil["scores", t, st, hh]),
                                   ("pe", mil.get(("last_trans", hh), 0))])
                        mil["exp", t, st, hh] = P.c["sc"]
                        mil["last_exp", hh] = P.c["sc"]
                        P.rel(pn, "sc")
                        if dbg and t == DBG_T and st == 0 and hh == 0:
                            P.do("gpsimd", lambda q: q.dma_start(out=dbg_t["d_expb"], in_=expb[:, 0:1024]),
                                 inc="sb10", amt=16, waits=[("sc", mil["exp", DBG_T, 0, 0])])
                            mil["dump_expb"] = 16
                        P.do("vector", lambda v, col=col, eo=eo: v.reduce_sum(
                            stats[:, SUMS + col * 8:SUMS + col * 8 + 8].rearrange("p (g o) -> p g o", g=8),
                            expb[:, eo:eo + 1024].rearrange("p (g m) -> p g m", g=8), axis=AX.X),
                            inc="ve", waits=[("sc", mil["exp", t, st, hh])])
                        mil["sums", t, st, hh] = P.c["ve"]
                        P.do("scalar", lambda sC, col=col, hh=hh: sC.activation(
                            out=stats[:, RCPC + hh * 8:RCPC + hh * 8 + 8],
                            in_=stats[:, SUMS + col * 8:SUMS + col * 8 + 8], func=AF.Ln),
                            inc="sc", waits=[("ve", mil["sums", t, st, hh])])
                        mil["rln", t, st, hh] = P.c["sc"]
                        P.do("scalar", lambda sC, hh=hh: sC.activation(
                            out=stats[:, RCPC + hh * 8:RCPC + hh * 8 + 8],
                            in_=stats[:, RCPC + hh * 8:RCPC + hh * 8 + 8],
                            func=AF.Exp, scale=-1.0), inc="sc",
                            waits=[("sc!", mil["rln", t, st, hh])])
                        mil["recip", t, st, hh] = P.c["sc"]
                        for h8 in range(8):
                            P.do("vector", lambda v, h8=h8, hh=hh, eo=eo: v.tensor_scalar_mul(
                                diag[:, eo + h8 * 128:eo + (h8 + 1) * 128], ident[:],
                                stats[:, RCPC + hh * 8 + h8:RCPC + hh * 8 + h8 + 1]),
                                inc="ve" if h8 == 7 else None,
                                waits=([("pe", mil.get(("last_trans", hh), 0)),
                                        ("sc", mil["recip", t, st, hh]),
                                        ("sb13", mil.get("dump_stats", 0))] if h8 == 0 else ()))
                        mil["diag", t, st, hh] = P.c["ve"]
                        if dbg and t == DBG_T and st == 0 and hh == 0:
                            P.do("gpsimd", lambda q: q.dma_start(out=dbg_t["d_diag"], in_=diag[:, 0:1024]),
                                 inc="sb12", amt=16, waits=[("ve", mil["diag", DBG_T, 0, 0])])
                            mil["dump_diag"] = 16
                            P.do("gpsimd", lambda q: q.dma_start(out=dbg_t["d_stats"], in_=stats[:]),
                                 inc="sb13", amt=16, waits=[("ve", mil["diag", DBG_T, 0, 0])])
                            mil["dump_stats"] = 16

                def attn_T(st, hh):
                    for hh in (hh,):
                        pst, pn = (ps45, "ps45") if hh == 0 else (ps67, "ps67")
                        eo = hh * 1024
                        w = P.acq(pn)
                        for h8 in range(8):
                            P.do("tensor", lambda tn, h8=h8, pst=pst, eo=eo: tn.matmul(
                                pst[:, h8 * 128:(h8 + 1) * 128],
                                expb[:, eo + h8 * 128:eo + (h8 + 1) * 128],
                                diag[:, eo + h8 * 128:eo + (h8 + 1) * 128],
                                start=True, stop=True),
                                inc="pe" if h8 == 7 else None,
                                waits=(w + [("ve", mil["diag", t, st, hh])]) if h8 == 0 else ())
                        mil["trans", t, st, hh] = P.c["pe"]
                        mil["last_trans", hh] = P.c["pe"]
                        P.do("vector", lambda v, pst=pst, eo=eo: v.tensor_copy(
                            out=attnT[:, eo:eo + 1024], in_=pst[:]),
                             inc="ve", waits=[("pe", mil["trans", t, st, hh]),
                                              ("sb11", mil.get("dump_attnT", 0))])
                        mil["attnT", t, st, hh] = P.c["ve"]
                        P.rel(pn, "ve")
                        if dbg and t == DBG_T and st == 0 and hh == 0:
                            P.do("gpsimd", lambda q: q.dma_start(out=dbg_t["d_attnT"], in_=attnT[:, 0:1024]),
                                 inc="sb11", amt=16, waits=[("ve", mil["attnT", DBG_T, 0, 0])])
                            mil["dump_attnT"] = 16
                def attn_O(st, hh):
                    for hh in (hh,):
                        eo = hh * 1024
                        bank = hh
                        w = P.acq("ps01_" + str(bank))
                        for i8 in range(8):
                            h8 = (i8 % 4) * 2 + i8 // 4
                            h = hh * 8 + h8
                            po = (h % 2) * 64
                            ci = (h8 // 2)
                            P.do("tensor", lambda tn, st=st, h=h, i8=i8, po=po, ci=ci, bank=bank, eo=eo: tn.matmul(
                                ps01[po:po + 64, bank * 512 + ci * 128: bank * 512 + (ci + 1) * 128],
                                vtok[:, st * 1024 + h * 64: st * 1024 + (h + 1) * 64],
                                attnT[:, eo + i8 * 128:eo + (i8 + 1) * 128],
                                start=True, stop=True),
                                inc="pe" if i8 == 7 else None,
                                waits=(w + [("ve", mil["attnT", t, st, hh]), ("ve", mil["v_evac", t, st]), ("sc", mil["qk_done", t])]) if i8 == 0 else ())
                        mil["out", t, st, hh] = P.c["pe"]
                        P.do("vector", lambda v, st=st, hh=hh, bank=bank: v.tensor_copy(
                            out=aoT[:, hh * 4 * 256:(hh + 1) * 4 * 256].rearrange(
                                "p (c j) -> p c j", j=256)[:, :, st * 128:st * 128 + 128],
                            in_=ps01[:, bank * 512:(bank + 1) * 512].rearrange("p (c j) -> p c j", j=128)),
                            inc="ve", waits=[("pe", mil["out", t, st, hh])])
                        mil["aoT", t, st, hh] = P.c["ve"]
                        P.rel("ps01_" + str(bank), "ve")

                def wout_st(st):
                    w = P.acq("ps23") + [("ve", mil["aoT", t, st, 1]), ("sc", mil["bcast"]), ("sp3", mil["rwout"])]
                    for half in range(2):
                        for k in range(KCH):
                            P.do("tensor", lambda tn, st=st, half=half, k=k: tn.matmul(
                                ps23[:, half * 512:(half + 1) * 512],
                                aoT[:, k * 256 + st * 128: k * 256 + st * 128 + 128],
                                RWout[:, k * E + half * 512: k * E + (half + 1) * 512],
                                start=(k == 0), stop=False),
                                waits=w if (half == 0 and k == 0) else ())
                        P.do("tensor", lambda tn, half=half: tn.matmul(
                            ps23[:, half * 512:(half + 1) * 512],
                            onesb[0:1, :], brow[0:1, half * 512:(half + 1) * 512],
                            start=False, stop=True),
                            inc="pe" if half == 1 else None)
                    mil["wout", t, st] = P.c["pe"]
                    P.do("vector", lambda v, st=st: v.tensor_tensor(
                        x1[:, x1off + st * 1024: x1off + (st + 1) * 1024],
                        ps23[:],
                        xf[:, (t % 2) * 2048 + st * 1024:(t % 2) * 2048 + (st + 1) * 1024],
                        ALU.add),
                        inc="ve",
                        waits=[("pe", mil["wout", t, st]), ("sx" + str(t % 2), mil["xload", t]),
                               ("sy" + str(t % 2), mil.get(("store", t - 2), 0))])
                    mil["resid1", t, st] = P.c["ve"]
                    P.rel("ps23", "ve")

                attn_A(0, 0)
                attn_A(0, 1)
                if t + 1 < n_tiles:
                    front_x(t + 1, wait_scores=mil["scores", t, 0, 1])
                attn_T(0, 0)
                attn_T(0, 1)
                attn_O(0, 0)
                attn_A(1, 0)
                attn_O(0, 1)
                attn_A(1, 1)
                wout_st(0)
                attn_T(1, 0)
                attn_T(1, 1)
                attn_O(1, 0)
                attn_O(1, 1)
                wout_st(1)

                if t + 2 < n_tiles:
                    P.do("gpsimd", lambda s, t=t: s.dma_start(
                        out=xf[:, (t % 2) * 2048:(t % 2) * 2048 + 2048].rearrange("p (st j) -> p st j", st=2),
                        in_=xsrc(t + 2)), inc="sx" + str(t % 2), amt=16,
                        waits=[("ve", mil["resid1", t, 1])])
                    mil["xload", t + 2] = P.c["sx" + str(t % 2)]

                if EARLY_A and t + 1 < n_tiles:
                    front_x(t + 1)
                    front_q(t + 1)

                if STAGE <= 7:
                    return
                if dbg and t == DBG_T:
                    P.do("gpsimd", lambda q: q.dma_start(out=dbg_t["d_aoT"], in_=aoT[:]),
                         inc="sb5", amt=16, waits=[("ve", mil["aoT", DBG_T, 1, 1])])
                    mil["dump_aoT"] = 16
                    P.do("gpsimd", lambda q: q.dma_start(out=dbg_t["d_x1a"], in_=x1[:, (DBG_T % 2) * 2048:(DBG_T % 2) * 2048 + 2048]),
                         inc="sb6", amt=16, waits=[("ve", mil["resid1", DBG_T, 1])])
                    mil["dump_x1a"] = 16

                if STAGE <= 7:
                    return
                S1, Q1, M1, MS1, V1, R1 = 36, 38, 40, 42, 44, 46
                P.do("vector", lambda v: v.reduce_sum(
                    stats[:, S1:S1 + 2].rearrange("p (g o) -> p g o", g=2),
                    x1[:, x1off:x1off + 2048].rearrange("p (g j) -> p g j", g=2),
                    axis=AX.X),
                    inc="ve", waits=[("ve!", mil["resid1", t, 1])])
                mil["lnsum", t] = P.c["ve"]
                for st in range(2):
                    pssq = ps45 if st == 0 else ps67
                    psname = "ps45" if st == 0 else "ps67"
                    xsl = x1[:, x1off + st * 1024: x1off + (st + 1) * 1024]
                    w = P.acq(psname)
                    P.do("scalar", lambda sC, xsl=xsl, pssq=pssq, st=st: sC.activation(
                        out=pssq[:], in_=xsl, func=AF.Square,
                        accum_out=stats[:, Q1 + st:Q1 + st + 1]),
                        inc="sc",
                        waits=w + [("ve", mil["resid1", t, st])])
                    mil["square", t, st] = P.c["sc"]
                    P.rel(psname, "sc")
                P.do("vector", lambda v: v.tensor_scalar_mul(
                    stats[:, M1:M1 + 2], stats[:, S1:S1 + 2], 1.0 / E),
                    inc="ve", waits=[("ve!", mil["lnsum", t])])
                mil["mean1", t] = P.c["ve"]
                P.do("vector", lambda v: v.tensor_tensor(
                    stats[:, MS1:MS1 + 2], stats[:, M1:M1 + 2],
                    stats[:, M1:M1 + 2], ALU.mult),
                    inc="ve", waits=[("ve!", mil["mean1", t])])
                mil["msq1", t] = P.c["ve"]
                P.do("vector", lambda v: v.scalar_tensor_tensor(
                    stats[:, V1:V1 + 2], stats[:, Q1:Q1 + 2], 1.0 / E,
                    stats[:, MS1:MS1 + 2], ALU.mult, ALU.subtract),
                    inc="ve", waits=[("sc", mil["square", t, 1]), ("ve!", mil["msq1", t])])
                mil["var1", t] = P.c["ve"]
                P.do("scalar", lambda sC: sC.activation(
                    out=stats[:, R1:R1 + 2], in_=stats[:, V1:V1 + 2], func=AF.Ln,
                    bias=stats[:, EPSC:EPSC + 1]),
                    inc="sc", waits=[("ve", mil["var1", t])])
                mil["lnln1", t] = P.c["sc"]
                P.do("scalar", lambda sC: sC.activation(
                    out=stats[:, R1:R1 + 2], in_=stats[:, R1:R1 + 2],
                    func=AF.Exp, scale=-0.5),
                    inc="sc", waits=[("sc!", mil["lnln1", t])])
                mil["rstd1", t] = P.c["sc"]
                for st in range(2):
                    xsl = x1[:, x1off + st * 1024: x1off + (st + 1) * 1024]
                    P.do("vector", lambda v, xsl=xsl, st=st: v.tensor_scalar(
                        xsl, xsl, stats[:, M1 + st:M1 + st + 1], stats[:, R1 + st:R1 + st + 1],
                        ALU.subtract, ALU.mult),
                        inc="ve", waits=([("sc", mil["rstd1", t]),
                                          ("sb6", mil.get("dump_x1a", 0))] if st == 0 else ()))
                    mil["norm1", t, st] = P.c["ve"]

                if dbg and t == DBG_T:
                    P.do("gpsimd", lambda q: q.dma_start(out=dbg_t["d_x1"], in_=x1[:, (DBG_T % 2) * 2048:(DBG_T % 2) * 2048 + 2048]),
                         inc="sb7", amt=16, waits=[("ve", mil["norm1", DBG_T, 1])])
                    mil["dump_x1"] = P.c["sb7"]

                for st in range(2):
                    xsl = x1[:, x1off + st * 1024: x1off + (st + 1) * 1024]
                    P.do("scalar", lambda sC, xsl=xsl, st=st: sC.activation(
                        out=aoT[:, st * 1024:(st + 1) * 1024], in_=xsl, func=AF.Copy),
                        inc="sc",
                        waits=[("ve", mil["norm1", t, st]), ("pe", mil["wout", t, 1]),
                               ("sb5", mil.get("dump_aoT", 0))])
                    mil["x1b", t, st] = P.c["sc"]
                    w = P.acq("ps23")
                    for c in range(KCH):
                        P.do("tensor", lambda tn, st=st, c=c: tn.matmul(
                            ps23[:, c * 128:(c + 1) * 128],
                            aoT[:, st * 1024 + c * 128: st * 1024 + (c + 1) * 128],
                            ident[:], start=True, stop=True),
                            inc="pe" if c == KCH - 1 else None,
                            waits=(w + [("sc", mil["x1b", t, st])]) if c == 0 else ())
                    mil["x1t_mm", t, st] = P.c["pe"]
                    l = p * 2 + st
                    P.do("vector", lambda v, l=l: v.tensor_copy(
                        out=x1T[:].rearrange("p (k j) -> p k j", j=512)[:, :, l * 128:l * 128 + 128],
                        in_=ps23[:].rearrange("p (k j) -> p k j", j=128)),
                        inc="ve",
                        waits=[("pe", mil["x1t_mm", t, st]),
                               ("pe", mil.get(("gu_done", s - 1), 0) if (t % 2 == 0 and st == 0) else 0)])
                    mil["x1t_evac", t, st] = P.c["ve"]
                    P.rel("ps23", "ve")

            def emit_wgu_set(s, g):
                gi = s * NGRP + g
                buf = gi % 2
                boff = buf * (2 * KCH * GC * 128)
                w = [("pe", mil.get(("gu_grp", gi - 2), 0)),
                     ("sps0", mil.get("gu_stores0", 0)), ("sps1", mil.get("gu_stores1", 0)),
                     ("sps2", mil.get("gu_stores2", 0)), ("sps3", mil.get("gu_stores3", 0)),
                     ("sb4", mil.get("dump_wgu", 0))]
                par = str(gi % 2)
                for wi, natT in ((0, wgT_s), (1, wuT_s)):
                    sem = ("swp" if wi == 0 else "svp") + par
                    P.do("sync", lambda q, natT=natT, g=g, wi=wi, boff=boff: q.dma_start(
                        out=wgu[:, boff + wi * (KCH * GC * 128):
                                boff + (wi + 1) * (KCH * GC * 128)].rearrange(
                                    "p (k j) -> p k j", j=GC * 128),
                        in_=natT[0:E, g * GC * 128:(g + 1) * GC * 128].rearrange(
                            "(k p) j -> p k j", p=128)),
                        inc=sem, amt=16, waits=w if wi == 0 else ())
                mil["wgu_set_sw", gi] = ("swp" + par, P.c["swp" + par])
                mil["wgu_set_sv", gi] = ("svp" + par, P.c["svp" + par])

            def emit_wd(s, k):
                w = [("pe", mil.get(("down_m", s, k - 4), mil.get(("down_done", s - 1), 0))),
                     ("sps0", mil.get("prep_stores0", 0)), ("sps1", mil.get("prep_stores1", 0)),
                     ("sb8", mil.get("dump_wdb", 0))]
                par = str(k % 4)
                P.do("scalar", lambda q, k=k: q.dma_start(
                    out=wdb[:, (k % 4) * E:(k % 4 + 1) * E],
                    in_=wdT_s[k * 128:(k + 1) * 128, 0:E]),
                    inc="sd" + par, amt=16, waits=w)
                mil["wd_dma", s, k] = ("sd" + par, P.c["sd" + par])

            def ffn(s):
                if dbg and s == DBG_S:
                    P.do("gpsimd", lambda q: q.dma_start(out=dbg_t["d_x1T"], in_=x1T[:]),
                         inc="sb3", amt=16, waits=[("ve", mil["x1t_evac", 2 * DBG_S + 1, 1])])
                    mil["dump_x1T"] = 16
                    P.do("gpsimd", lambda q: q.dma_start(out=dbg_t["d_wgu"], in_=wgu[:]),
                         inc="sb4", amt=16,
                         waits=[mil["wgu_set_sw", DBG_S * NGRP], mil["wgu_set_sv", DBG_S * NGRP],
                                mil["wgu_set_sw", DBG_S * NGRP + 1], mil["wgu_set_sv", DBG_S * NGRP + 1]])
                    mil["dump_wgu"] = 16

                for g in range(NGRP):
                    gi = s * NGRP + g
                    buf = gi % 2
                    boff = buf * (2 * KCH * GC * 128)
                    for mi_ in range(GC):
                        m = g * GC + mi_
                        pair = ps45 if m % 2 == 0 else ps67
                        pname = "ps45" if m % 2 == 0 else "ps67"
                        w = P.acq(pname) + [
                            mil["wgu_set_sw", gi], mil["wgu_set_sv", gi],
                            ("ve", mil["x1t_evac", 2 * s + 1, 1]),
                            ("ve", mil.get(("hmul", s, m - 2), 0)),
                            ("sc", mil.get(("silu", s, m - 2), 0))]
                        for wi in range(2):
                            for k in range(KCH):
                                P.do("tensor", lambda tn, wi=wi, k=k, mi_=mi_, boff=boff, pair=pair: tn.matmul(
                                    pair[:, wi * 512:(wi + 1) * 512],
                                    wgu[:, boff + wi * (KCH * GC * 128) + k * (GC * 128) + mi_ * 128:
                                        boff + wi * (KCH * GC * 128) + k * (GC * 128) + (mi_ + 1) * 128],
                                    x1T[:, k * 512:(k + 1) * 512],
                                    start=(k == 0), stop=(k == KCH - 1)),
                                    inc="pe" if k == KCH - 1 else None,
                                    waits=w if (wi == 0 and k == 0) else ())
                            if wi == 0:
                                mil["gate", s, m] = P.c["pe"]
                            else:
                                mil["up", s, m] = P.c["pe"]
                        P.do("scalar", lambda sC, m=m, pair=pair: sC.activation(
                            out=silu2[:, (m % 2) * 512:(m % 2 + 1) * 512],
                            in_=pair[:, 0:512], func=AF.Silu, scale=S127),
                            inc="sc",
                            waits=[("pe", mil["gate", s, m]), ("ve", mil.get(("hmul", s, m - 2), 0))])
                        mil["silu", s, m] = P.c["sc"]
                        P.do("vector", lambda v, m=m, pair=pair: v.tensor_tensor(
                            hT[:, m * 512:(m + 1) * 512],
                            silu2[:, (m % 2) * 512:(m % 2 + 1) * 512],
                            pair[:, 512:1024], ALU.mult),
                            inc="ve",
                            waits=[("sc", mil["silu", s, m]), ("pe", mil["up", s, m]),
                                   ("pe", mil.get(("down_done", s - 1), 0))])
                        mil["hmul", s, m] = P.c["ve"]
                    mil["gu_grp", gi] = P.c["pe"]
                    if g + 2 < NGRP:
                        emit_wgu_set(s, g + 2)
                mil["gu_done", s] = P.c["pe"]

                if dbg and s == DBG_S:
                    P.do("gpsimd", lambda q: q.dma_start(out=dbg_t["d_hT"], in_=hT[:]),
                         inc="sb7", amt=16, waits=[("ve", mil["hmul", DBG_S, MCH - 1])])
                    mil["dump_hT"] = P.c["sb7"]
                    P.do("gpsimd", lambda q: q.dma_start(out=dbg_t["d_wdb"], in_=wdb[:]),
                         inc="sb8", amt=16,
                         waits=([("spt0", mil["wdt", 2]), ("spt1", mil["wdt", 3])]
                                if s == 0 else [("sd0", 16), ("sd1", 16)]))
                    mil["dump_wdb"] = 16

                if STAGE <= 9:
                    return
                psl = [ps01, ps23, ps45, ps67]
                for m in range(MCH):
                    wslot = (m // 2) % 2
                    coff = (m % 2) * E
                    w = [(("spt" + str(m % 2), mil["wdt", m]) if s == 0
                          else mil["wd_dma", s, m]), ("ve", mil["hmul", s, m])]
                    if m == 0:
                        w += [("ve", mil["hmul", s, MCH - 1]), ("sb8", mil.get("dump_wdb", 0))]
                        w += P.acq("ps01_0") + P.acq("ps01_1") + P.acq("ps23") + P.acq("ps45") + P.acq("ps67")
                        w += [("ve", mil["aoT", 2 * s + 1, 1, 1]), ("ve", mil["x1t_evac", 2 * s + 1, 1])]
                    for l in range(4):
                        for half in range(2):
                            P.do("tensor", lambda tn, m=m, l=l, half=half, wslot=wslot, coff=coff: tn.matmul(
                                psl[l][:, half * 512:(half + 1) * 512],
                                hT[:, m * 512 + l * 128: m * 512 + l * 128 + 128],
                                wdb[:, wslot * 2 * E + coff + half * 512: wslot * 2 * E + coff + (half + 1) * 512],
                                start=(m == 0), stop=(m == MCH - 1)),
                                inc="pe" if (l == 3 and half == 1) else None,
                                waits=w if (l == 0 and half == 0) else ())
                    mil["down_m", s, m] = P.c["pe"]
                    if m + 4 < MCH:
                        if s == 0:
                            emit_wdt(m + 4, extra=[("pe", mil["down_m", 0, m])])
                            emit_wdst(m + 4)
                        else:
                            emit_wd(s, m + 4)
                mil["down_done", s] = P.c["pe"]
                if s == 0:
                    mil["prep_stores0"] = P.c["sps0"]
                    mil["prep_stores1"] = P.c["sps1"]

                for l in range(4):
                    xsl = x1[:, l * 1024:(l + 1) * 1024]
                    P.do("vector", lambda v, xsl=xsl, l=l: v.scalar_tensor_tensor(
                        xsl, psl[l][:], S127SQ, xsl, ALU.mult, ALU.add),
                        inc="ve", waits=([("pe", mil["down_done", s]),
                                          ("sb7", mil.get("dump_hT", 0))] if l == 0 else ()))
                    mil["resid2", s, l] = P.c["ve"]
                    if l == 0:
                        P.rel("ps01_0", "ve")
                        P.rel("ps01_1", "ve")
                    if dbg and s == DBG_S and l == 1:
                        P.do("gpsimd", lambda q: q.dma_start(out=dbg_t["d_t2"], in_=x1[:, 2048:4096]),
                             inc="sb9", amt=16, waits=[("ve", mil["resid2", DBG_S, 1])])
                        mil["dump_t2"] = 16
                for l in range(4):
                    xsl = x1[:, l * 1024:(l + 1) * 1024]
                    dump = [ps45, ps23, ps45, ps67][l]
                    P.do("scalar", lambda sC, xsl=xsl, dump=dump, l=l: sC.activation(
                        out=dump[:], in_=xsl, func=AF.Square,
                        accum_out=stats[:, Q2c + l:Q2c + l + 1]),
                        inc="sc", waits=[("ve", mil["resid2", s, max(l, 2) if l != 1 else 1])]
                        + ([("sc!", mil["square2", s, 0])] if l == 2 else []))
                    mil["square2", s, l] = P.c["sc"]
                    if l >= 1:
                        P.rel(["ps01_0", "ps23", "ps45", "ps67"][l], "sc")
                P.do("vector", lambda v: v.reduce_sum(
                    stats[:, S2c:S2c + 4].rearrange("p (g o) -> p g o", g=4),
                    x1[:].rearrange("p (g j) -> p g j", g=4), axis=AX.X),
                    inc="ve", waits=[("ve!", mil["resid2", s, 3])])
                mil["sum2", s] = P.c["ve"]
                P.do("vector", lambda v: v.tensor_scalar_mul(
                    stats[:, M2c:M2c + 4], stats[:, S2c:S2c + 4], 1.0 / E),
                    inc="ve", waits=[("ve!", mil["sum2", s])])
                mil["mean2", s] = P.c["ve"]
                P.do("vector", lambda v: v.tensor_tensor(
                    stats[:, MS2c:MS2c + 4], stats[:, M2c:M2c + 4],
                    stats[:, M2c:M2c + 4], ALU.mult),
                    inc="ve", waits=[("ve!", mil["mean2", s])])
                mil["msq2", s] = P.c["ve"]
                P.do("vector", lambda v: v.scalar_tensor_tensor(
                    stats[:, V2c:V2c + 4], stats[:, Q2c:Q2c + 4], 1.0 / E,
                    stats[:, MS2c:MS2c + 4], ALU.mult, ALU.subtract),
                    inc="ve", waits=[("sc", mil["square2", s, 3]), ("ve!", mil["msq2", s])])
                mil["var2", s] = P.c["ve"]
                P.do("scalar", lambda sC: sC.activation(
                    out=stats[:, R2c:R2c + 4], in_=stats[:, V2c:V2c + 4], func=AF.Ln,
                    bias=stats[:, EPSC:EPSC + 1]),
                    inc="sc", waits=[("ve", mil["var2", s])])
                mil["lnln2", s] = P.c["sc"]
                P.do("scalar", lambda sC: sC.activation(
                    out=stats[:, R2c:R2c + 4], in_=stats[:, R2c:R2c + 4],
                    func=AF.Exp, scale=-0.5),
                    inc="sc", waits=[("sc!", mil["lnln2", s])])
                mil["rstd2", s] = P.c["sc"]
                for l in range(4):
                    xsl = x1[:, l * 1024:(l + 1) * 1024]
                    P.do("vector", lambda v, xsl=xsl, l=l: v.tensor_scalar(
                        xsl, xsl, stats[:, M2c + l:M2c + l + 1], stats[:, R2c + l:R2c + l + 1],
                        ALU.subtract, ALU.mult),
                        inc="ve", waits=([("sc", mil["rstd2", s]),
                                          ("sb9", mil.get("dump_t2", 0))] if l == 0 else ()))
                    mil["norm2", s, l] = P.c["ve"]

                for p in range(2):
                    t = 2 * s + p
                    P.do("gpsimd", lambda sq, t=t, p=p: sq.dma_start(
                        out=y_out[t * TTOK:(t + 1) * TTOK, :].rearrange("(st pp) j -> pp st j", pp=128),
                        in_=x1[:, p * 2048:(p + 1) * 2048].rearrange("q (st j) -> q st j", st=2)),
                        inc="sy" + str(t % 2), amt=16,
                        waits=[("ve", mil["norm2", s, 2 * p + 1]), ("ve", mil["norm2", s, 2 * p])])
                    mil["store", t] = P.c["sy" + str(t % 2)]

            front_x(0)
            front_q(0)
            emit_prep_casts()
            for s in range(n_tiles // 2):
                tile_front(2 * s)
                if STAGE <= 8:
                    break
                if not EARLY_A:
                    front_q(2 * s + 1)
                if s == 0:
                    emit_prep_casts2()
                tile_front(2 * s + 1)
                if not EARLY_A and 2 * s + 2 < n_tiles:
                    front_q(2 * s + 2)
                ffn(s)
                if STAGE <= 10:
                    break

            for nm in SEM_PROD:
                if P.c[nm] > 0 and SEM_PROD[nm] != "sync":
                    P.do("sync", lambda sq, nm=nm, v=P.c[nm]: sq.wait_ge(sems[nm], v))
            for nm in SEM_PROD:
                if P.c[nm] > 0 and SEM_PROD[nm] == "sync":
                    P.do("sync", lambda sq, nm=nm, v=P.c[nm]: sq.wait_ge(sems[nm], v))

        @block.tensor
        def _(tn):
            body(Prog("tensor", {"tensor": tn}, sems))

        @block.vector
        def _(v):
            body(Prog("vector", {"vector": v}, sems))

        @block.scalar
        def _(sC):
            body(Prog("scalar", {"scalar": sC}, sems))

        @block.sync
        def _(sq):
            body(Prog("sync", {"sync": sq}, sems))

        @block.gpsimd
        def _(g):
            body(Prog("gpsimd", {"gpsimd": g}, sems))

    return nc


def make_host_inputs(inputs, shard):
    b_qkv = inputs["L0_attn_b_qkv"]
    mask = np.zeros((128, 128), dtype=np.float32)
    mask[:64, 64:] = -1e9
    mask[64:, :64] = -1e9
    BIG = 32768.0
    SHIFT = 20.0
    bA = (np.arange(128) < 64).astype(np.float32)
    bB = 1.0 - bA
    augc = np.zeros((128, 256), dtype=np.float32)
    for base in (0, 64):
        augc[base + 0, 0:128] = 1.0
        augc[base + 1, 0:128] = bA
        augc[base + 2, 0:128] = bB
        augc[base + 3, 0:128] = 1.0
        augc[base + 0, 128:256] = -BIG
        augc[base + 1, 128:256] = BIG * bA
        augc[base + 2, 128:256] = BIG * bB
        augc[base + 3, 128:256] = -SHIFT
    return dict(
        x=np.ascontiguousarray(shard),
        wqkv=inputs["L0_attn_W_qkv"],
        wout=inputs["L0_attn_W_out"],
        wg=inputs["L0_ffn_w_gate"],
        wu=inputs["L0_ffn_w_up"],
        wd=inputs["L0_ffn_w_down"],
        bqkvT=np.ascontiguousarray(b_qkv.reshape(3 * KCH, 128).T),
        bout=np.ascontiguousarray(inputs["L0_attn_b_out"][None, :]),
        bv=np.ascontiguousarray(b_qkv[None, 2 * E:3 * E]),
        ident=np.eye(128, dtype=BF),
        ones=np.ones((1, 128), dtype=BF),
        mask=mask,
        augc=augc.astype(BF),
    )


def kernel(**inputs):
    x = np.asarray(inputs["L0_attn_x"])
    n = 8
    inputs = {k: np.ascontiguousarray(np.asarray(v)) for k, v in inputs.items()}
    nc = build_nc(NT)
    in_maps = [make_host_inputs(inputs, x[i]) for i in range(n)]
    res = run_bass_kernel_spmd(nc, in_maps, core_ids=list(range(n)))
    out = np.stack([np.asarray(r["y"], dtype=np.float32) for r in res.results], axis=0)
    return out


# version 37
# speedup vs baseline: 1.0577x; 1.0577x over previous
import numpy as np
import ml_dtypes
from contextlib import ExitStack

import concourse.bass as bass
import concourse.mybir as mybir
from concourse.bass_utils import run_bass_kernel_spmd

F32 = mybir.dt.float32
BF16 = mybir.dt.bfloat16
I32 = mybir.dt.int32
AF = mybir.ActivationFunctionType
ALU = mybir.AluOpType
AX = mybir.AxisListType
BF = ml_dtypes.bfloat16

E = 1024
KCH = 8
H = 16
HD = 64
HID = 3072
MCH = 24
TTOK = 256
NT = 16
GC = 3
NGRP = MCH // GC
S127 = float(np.float32(1.0 / 127.0))
S127SQ = float(np.float32(1.0 / 127.0) * np.float32(1.0 / 127.0))
QSCALE = 0.125
LNEPS = 1e-5

ENGINES = ("tensor", "vector", "scalar", "sync", "gpsimd")
SEM_PROD = {
    "pe": "tensor", "ve": "vector", "sc": "scalar",
    "sgq0": "gpsimd", "sgq1": "gpsimd", "sgq2": "gpsimd", "sgq3": "gpsimd",
    "sgo0": "gpsimd", "sgo1": "gpsimd",
    "sgg0": "gpsimd", "sgg1": "gpsimd", "sgg2": "gpsimd", "sgg3": "gpsimd",
    "sgu0": "gpsimd", "sgu1": "gpsimd", "sgu2": "gpsimd", "sgu3": "gpsimd",
    "sgd0": "gpsimd", "sgd1": "gpsimd", "sgd2": "gpsimd", "sgd3": "gpsimd",
    "sgx": "gpsimd",
    "sp1": "gpsimd", "sp2": "sync", "sp3": "sync", "sp4": "gpsimd",
    "sx0": "gpsimd", "sx1": "gpsimd",
    "spt2": "sync", "spt3": "sync", "sps2": "gpsimd", "sps3": "gpsimd",
    "swp0": "sync", "swp1": "sync",
    "svp0": "sync", "svp1": "sync",
    "sd0": "scalar", "sd1": "scalar", "sd2": "scalar", "sd3": "scalar",
    "sy0": "gpsimd", "sy1": "gpsimd",
    "sb0": "gpsimd", "sb1": "gpsimd", "sb2": "gpsimd", "sb3": "gpsimd",
    "sb4": "gpsimd", "sb5": "gpsimd", "sb6": "gpsimd", "sb7": "gpsimd",
    "sb8": "gpsimd", "sb9": "gpsimd", "sb10": "gpsimd", "sb11": "gpsimd",
    "sb12": "gpsimd", "sb13": "gpsimd",
    "spt0": "sync", "spt1": "sync",
    "sps0": "gpsimd", "sps1": "gpsimd",
}


class Prog:

    def __init__(self, live, eng, sems):
        self.live = live
        self.eng = eng
        self.sems = sems
        self.c = {k: 0 for k in SEM_PROD}
        self._waited = {k: 0 for k in SEM_PROD}
        self.guards = {}

    def do(self, engname, emit=None, inc=None, amt=1, waits=()):
        if engname == self.live:
            eng = self.eng[engname]
            for (s, v) in waits:
                force = s.endswith("!")
                if force:
                    s = s[:-1]
                if v <= 0 or (SEM_PROD[s] == engname and not force):
                    continue
                if v <= self._waited[s]:
                    continue
                eng.wait_ge(self.sems[s], v)
                self._waited[s] = v
            inst = emit(eng)
            if inc is not None:
                inst.then_inc(self.sems[inc], amt)
        if inc is not None:
            self.c[inc] += amt

    def acq(self, name):
        g = self.guards.pop(name, [])
        return g

    def rel(self, name, sem):
        self.guards.setdefault(name, []).append((sem, self.c[sem]))


STAGE = 99
EARLY_A = 0
DBG_T = 0
DBG_S = 0


def build_nc(n_tiles=NT, dbg=False):
    nc = bass.Bass("TRN2", target_bir_lowering=False, detect_race_conditions=False)
    T4 = n_tiles * TTOK

    x_in = nc.dram_tensor("x", [T4, E], F32, kind="ExternalInput").ap()
    wqkv = nc.dram_tensor("wqkv", [3 * E, E], F32, kind="ExternalInput").ap()
    wout = nc.dram_tensor("wout", [E, E], F32, kind="ExternalInput").ap()
    wg_i = nc.dram_tensor("wg", [HID, E], I32, kind="ExternalInput").ap()
    wu_i = nc.dram_tensor("wu", [HID, E], I32, kind="ExternalInput").ap()
    wd_i = nc.dram_tensor("wd", [E, HID], I32, kind="ExternalInput").ap()
    bqkvT_in = nc.dram_tensor("bqkvT", [128, 3 * KCH], F32, kind="ExternalInput").ap()
    bout_in = nc.dram_tensor("bout", [1, E], F32, kind="ExternalInput").ap()
    bv_in = nc.dram_tensor("bv", [1, E], F32, kind="ExternalInput").ap()
    ident_in = nc.dram_tensor("ident", [128, 128], BF16, kind="ExternalInput").ap()
    ones_in = nc.dram_tensor("ones", [1, 128], BF16, kind="ExternalInput").ap()
    mask_in = nc.dram_tensor("mask", [128, 128], F32, kind="ExternalInput").ap()
    augc_in = nc.dram_tensor("augc", [128, 256], BF16, kind="ExternalInput").ap()

    y_out = nc.dram_tensor("y", [T4, E], F32, kind="ExternalOutput").ap()

    dbg_t = {}
    if dbg:
        for nm, shape, dt_ in (("d_xT", [128, 2048], BF16), ("d_qkT", [128, 4096], BF16),
                               ("d_vtok", [128, 2048], BF16), ("d_x1T", [128, 4096], BF16),
                               ("d_wgu", [128, 4 * KCH * GC * 128], BF16), ("d_aoT", [128, 2048], BF16),
                               ("d_x1a", [128, 2048], F32), ("d_x1", [128, 2048], F32),
                               ("d_hT", [128, 12288], BF16), ("d_wdb", [128, 2048], BF16),
                               ("d_t2", [128, 2048], F32), ("d_expb", [128, 1024], BF16),
                               ("d_attnT", [128, 1024], BF16), ("d_diag", [128, 1024], BF16),
                               ("d_stats", [128, 64], F32)):
            dbg_t[nm] = nc.dram_tensor(nm, shape, dt_, kind="ExternalOutput").ap()

    wout_nat = nc.dram_tensor("wout_nat", [E, E], BF16, kind="Internal").ap()
    wg_nat = nc.dram_tensor("wg_nat", [HID, E], BF16, kind="Internal").ap()
    wu_nat = nc.dram_tensor("wu_nat", [HID, E], BF16, kind="Internal").ap()
    wd_nat = nc.dram_tensor("wd_nat", [E, HID], BF16, kind="Internal").ap()
    wgT_s = nc.dram_tensor("wgT_s", [E, HID], BF16, kind="Internal").ap()
    wuT_s = nc.dram_tensor("wuT_s", [E, HID], BF16, kind="Internal").ap()
    wdT_s = nc.dram_tensor("wdT_s", [HID, E], BF16, kind="Internal").ap()

    with ExitStack() as ctx:
        en = ctx.enter_context
        RWqkv = en(nc.sbuf_tensor("RWqkv", [128, KCH * 3 * E], BF16))
        RWout = en(nc.sbuf_tensor("RWout", [128, KCH * E], BF16))
        wgu = en(nc.sbuf_tensor("wgu", [128, 2 * 2 * KCH * GC * 128], BF16))
        wdb = en(nc.sbuf_tensor("wdb", [128, 4 * E], BF16))
        xf = en(nc.sbuf_tensor("xf", [128, 2 * 2048], F32))
        xb = en(nc.sbuf_tensor("xb", [128, 2048], BF16))
        xT = en(nc.sbuf_tensor("xT", [128, 2048], BF16))
        qkT = en(nc.sbuf_tensor("qkT", [128, 4096], BF16))
        vtok = en(nc.sbuf_tensor("vtok", [128, 2048], BF16))
        expb = en(nc.sbuf_tensor("expb", [128, 2048], BF16))
        diag = en(nc.sbuf_tensor("diag", [128, 2048], BF16))
        attnT = en(nc.sbuf_tensor("attnT", [128, 2048], BF16))
        aoT = en(nc.sbuf_tensor("aoT", [128, 2048], BF16))
        x1 = en(nc.sbuf_tensor("x1", [128, 4096], F32))
        x1T = en(nc.sbuf_tensor("x1T", [128, 4096], BF16))
        hT = en(nc.sbuf_tensor("hT", [128, MCH * 512], BF16))
        silu2 = en(nc.sbuf_tensor("silu2", [128, 1024], BF16))
        stats = en(nc.sbuf_tensor("stats", [128, 80], F32))
        bqkvT = en(nc.sbuf_tensor("bqkvT_s", [128, 3 * KCH], F32))
        ident = en(nc.sbuf_tensor("ident_s", [128, 128], BF16))
        augcS = en(nc.sbuf_tensor("augcS", [128, 256], BF16))
        onesb = en(nc.sbuf_tensor("onesb", [1, 128], BF16))
        stgb = en(nc.sbuf_tensor("stgb", [1, E], F32))
        brow = en(nc.sbuf_tensor("brow", [1, 2 * E], BF16))
        ps01 = en(nc.psum_tensor("ps01", [128, 1024], F32))
        ps23 = en(nc.psum_tensor("ps23", [128, 1024], F32))
        ps45 = en(nc.psum_tensor("ps45", [128, 1024], F32))
        ps67 = en(nc.psum_tensor("ps67", [128, 1024], F32))
        PS = {"01": ps01, "23": ps23, "45": ps45, "67": ps67}

        sems = {k: en(nc.semaphore(k)) for k in SEM_PROD}
        global LAST_SEMS
        LAST_SEMS = {k: s.num for k, s in sems.items()}
        block = en(nc.Block())

        NEGMAX = 0
        SUMS = 4
        RCPC = 52
        RCP = 36
        LNS = 36
        S2c, Q2c, M2c, MS2c, V2c, R2c, EPSC = 68, 72, 36, 40, 44, 48, 76

        def body(P: Prog):
            mil = {}

            def xsrc(t):
                return x_in[t * TTOK:(t + 1) * TTOK, :].rearrange("(st pp) j -> pp st j", pp=128)

            for dst, src in ((ident[:], ident_in), (onesb[:], ones_in),
                             (bqkvT[:], bqkvT_in), (stgb[:], bout_in), (augcS[:], augc_in)):
                P.do("gpsimd", lambda s, d=dst, sr=src: s.dma_start(out=d, in_=sr), inc="sp1", amt=16)
            mil["consts"] = P.c["sp1"]
            for t in range(min(2, n_tiles)):
                P.do("gpsimd", lambda s, t=t: s.dma_start(
                    out=xf[:, (t % 2) * 2048:(t % 2) * 2048 + 2048].rearrange("p (st j) -> p st j", st=2),
                    in_=xsrc(t)), inc="sx" + str(t % 2), amt=16)
                mil["xload", t] = P.c["sx" + str(t % 2)]
            P.do("scalar", lambda s: s.activation(out=brow[0:1, 0:E], in_=stgb[0:1, :], func=AF.Copy),
                 inc="sc", waits=[("sp1", mil["consts"])])
            mil["bcast1"] = P.c["sc"]
            P.do("gpsimd", lambda s: s.dma_start(out=stgb[:], in_=bv_in), inc="sp4", amt=16,
                 waits=[("sc", mil["bcast1"])])
            mil["bvload"] = P.c["sp4"]
            P.do("scalar", lambda s: s.activation(out=brow[0:1, E:2 * E], in_=stgb[0:1, :], func=AF.Copy),
                 inc="sc", waits=[("sp4", mil["bvload"])])
            mil["bcast"] = P.c["sc"]
            P.do("vector", lambda v: v.tensor_scalar_mul(bqkvT[:, 0:KCH], bqkvT[:, 0:KCH], QSCALE),
                 inc="ve", waits=[("sp1", mil["consts"])])
            mil["bqscale"] = P.c["ve"]
            P.do("vector", lambda v: v.memset(stats[:, EPSC:EPSC + 1], LNEPS), inc="ve")
            mil["epsc"] = P.c["ve"]
            for cc in range(2):
                P.do("gpsimd", lambda g, cc=cc: g.dma_start(
                    out=wout_nat[:, cc * 512:(cc + 1) * 512],
                    in_=wout[:, cc * 512:(cc + 1) * 512]), inc="sgo" + str(cc), amt=16)
            def emit_qkv_chunk(c):
                sl = c % 4
                P.do("gpsimd", lambda g, c=c, sl=sl: g.dma_start(
                    out=x1[:, sl * 1024:(sl + 1) * 1024],
                    in_=wqkv[c * 128:(c + 1) * 128, :]),
                    inc="sgq" + str(sl), amt=16,
                    waits=[("ve", mil.get(("qkvcast", c - 4), 0))])
                mil["qkvld", c] = P.c["sgq" + str(sl)]
                P.do("vector", lambda v, c=c, sl=sl: v.tensor_copy(
                    out=hT[:, sl * 1024:(sl + 1) * 1024],
                    in_=x1[:, sl * 1024:(sl + 1) * 1024]),
                    inc="ve",
                    waits=[("sgq" + str(sl), mil["qkvld", c]),
                           ("pe", mil.get(("qkvmm", c - 4), 0))])
                mil["qkvcast", c] = P.c["ve"]
                pname = "ps23" if c % 2 == 0 else "ps45"
                pair = ps23 if c % 2 == 0 else ps45
                w = P.acq(pname) + [("ve", mil["qkvcast", c]), ("sp1", mil["consts"])]
                for k in range(KCH):
                    P.do("tensor", lambda tn, k=k, sl=sl, pair=pair: tn.matmul(
                        pair[:, k * 128:(k + 1) * 128],
                        hT[:, sl * 1024 + k * 128: sl * 1024 + (k + 1) * 128],
                        ident[:], start=True, stop=True),
                        inc="pe" if k == KCH - 1 else None,
                        waits=w if k == 0 else ())
                mil["qkvmm", c] = P.c["pe"]

            def emit_qkv_evac(c):
                pname = "ps23" if c % 2 == 0 else "ps45"
                pair = ps23 if c % 2 == 0 else ps45
                P.do("vector", lambda v, c=c, pair=pair: v.tensor_copy(
                    out=RWqkv[:].rearrange("p (k j) -> p k j", j=3 * E)[:, :, c * 128:(c + 1) * 128],
                    in_=pair[:].rearrange("p (k j) -> p k j", j=128)),
                    inc="ve", waits=[("pe", mil["qkvmm", c])])
                mil["qkvevac", c] = P.c["ve"]
                P.rel(pname, "ve")

            for c in range(3 * KCH):
                if c >= 2:
                    emit_qkv_evac(c - 2)
                emit_qkv_chunk(c)
            emit_qkv_evac(3 * KCH - 2)
            emit_qkv_evac(3 * KCH - 1)
            mil["rqkv"] = P.c["ve"]

            for k in range(KCH):
                P.do("sync", lambda s, k=k: s.dma_start(
                    out=RWout[:, k * E:(k + 1) * E],
                    in_=wout_nat[:, k * 128:(k + 1) * 128], transpose=True),
                    inc="sp3", amt=16, waits=[("sgo" + str(k // 4), 16)])
            mil["rwout"] = P.c["sp3"]

            cast_chunks = [
                ("sgg0", lambda g: g.dma_start(out=wg_nat, in_=wg_i)),
                ("sgu0", lambda g: g.dma_start(out=wu_nat, in_=wu_i)),
            ]
            wd_casts = [("sgd0", lambda g: g.dma_start(out=wd_nat, in_=wd_i))]

            pjobs = []
            for nat, dstT in ((wg_nat, wgT_s), (wu_nat, wuT_s)):
                for k in range(KCH):
                    pjobs.append((nat, dstT, k))

            def job_chunkidx(j):
                nat, dstT, k = pjobs[j]
                return 0 if dstT is wgT_s else 1

            def emit_ptrans(j):
                nat, dstT, k = pjobs[j]
                par = j % 4
                stg = wgu[:, par * HID: (par + 1) * HID]
                castsem = ("sgg0" if dstT is wgT_s else "sgu0")
                P.do("sync", lambda q, nat=nat, k=k, stg=stg: q.dma_start(
                    out=stg, in_=nat[:, k * 128:(k + 1) * 128], transpose=True),
                    inc="spt" + str(par), amt=16,
                    waits=[(castsem, 16), ("sps" + str(par), mil.get(("pstore", j - 4), 0))])
                mil["ptrans", j] = P.c["spt" + str(par)]

            def emit_pstore(j):
                nat, dstT, k = pjobs[j]
                par = j % 4
                stg = wgu[:, par * HID: (par + 1) * HID]
                P.do("gpsimd", lambda q, dstT=dstT, k=k, stg=stg: q.dma_start(
                    out=dstT[k * 128:(k + 1) * 128, 0:HID], in_=stg),
                    inc="sps" + str(par), amt=16,
                    waits=[("spt" + str(par), mil["ptrans", j])])
                mil["pstore", j] = P.c["sps" + str(par)]

            def wdb_slot(k):
                base = ((k // 2) % 2) * 2 * E + (k % 2) * E
                return wdb[:, base:base + E]

            def emit_wdt(k, extra=()):
                par = str(k % 2)
                w = [("sgd0", 16)] + list(extra)
                if k >= 2:
                    w.append(("spt" + par + "!", mil["wdt", k - 2]))
                if k >= 4:
                    w.append(("sps" + par, mil["wdst", k - 4]))
                P.do("sync", lambda q, k=k: q.dma_start(
                    out=wdb_slot(k), in_=wd_nat[:, k * 128:(k + 1) * 128], transpose=True),
                    inc="spt" + par, amt=16, waits=w)
                mil["wdt", k] = P.c["spt" + par]

            def emit_wdst(k):
                par = str(k % 2)
                w = [("spt" + par, mil["wdt", k])]
                if k >= 2:
                    w.append(("sps" + par + "!", mil["wdst", k - 2]))
                P.do("gpsimd", lambda q, k=k: q.dma_start(
                    out=wdT_s[k * 128:(k + 1) * 128, 0:E], in_=wdb_slot(k)),
                    inc="sps" + par, amt=16, waits=w)
                mil["wdst", k] = P.c["sps" + par]

            def emit_prep_casts():
                for sem, fn in cast_chunks:
                    P.do("gpsimd", fn, inc=sem, amt=16)

            def emit_prep_casts2():
                for sem, fn in wd_casts:
                    P.do("gpsimd", fn, inc=sem, amt=16)
                for j0 in range(4):
                    emit_ptrans(j0)
                for j in range(len(pjobs)):
                    emit_pstore(j)
                    if j + 4 < len(pjobs):
                        emit_ptrans(j + 4)
                for q_ in range(4):
                    mil["gu_stores" + str(q_)] = P.c["sps" + str(q_)]
                emit_wgu_set(0, 0)
                emit_wgu_set(0, 1)
                for k in range(4):
                    emit_wdt(k)
                for k in range(4):
                    emit_wdst(k)


            def front_x(t, wait_scores=0):
                P.do("vector", lambda v, t=t: v.tensor_copy(
                    out=xb[:], in_=xf[:, (t % 2) * 2048:(t % 2) * 2048 + 2048]),
                    inc="ve",
                    waits=[("sx" + str(t % 2), mil["xload", t]),
                           ("pe", mil.get(("xt_done", t - 1), 0))])
                mil["xbcopy", t] = P.c["ve"]
                for st in range(2):
                    w = P.acq("ps01_0") + P.acq("ps01_1") + [("ve", mil["xbcopy", t]), ("sp1", mil["consts"])]
                    if st == 0 and wait_scores:
                        w.append(("pe", wait_scores))
                    for c in range(KCH):
                        P.do("tensor", lambda tn, st=st, c=c: tn.matmul(
                            ps01[:, c * 128:(c + 1) * 128],
                            xb[:, st * 1024 + c * 128: st * 1024 + (c + 1) * 128],
                            ident[:], start=True, stop=True),
                            inc="pe" if c == KCH - 1 else None,
                            waits=w if c == 0 else ())
                    mil["xt_mm", t, st] = P.c["pe"]
                    P.do("vector", lambda v, st=st: v.tensor_copy(
                        out=xT[:].rearrange("p (k j) -> p k j", j=256)[:, :, st * 128:st * 128 + 128],
                        in_=ps01[:].rearrange("p (k j) -> p k j", j=128)),
                        inc="ve", waits=[("pe", mil["xt_mm", t, st]),
                                         ("sb0", mil.get("dump_xT", 0))])
                    mil["xt_evac", t, st] = P.c["ve"]
                    P.rel("ps01_0", "ve")
                    P.rel("ps01_1", "ve")
                mil["xt_done", t] = P.c["pe"]


            def front_q(t):
                for c in range(16):
                    bank = c % 2
                    w = [("ve", mil["xt_evac", t, 1]), ("ve", mil["rqkv"]),
                         ("sc", mil.get(("qk_evac", t, c - 2), 0))]
                    if c < 2:
                        w += P.acq("ps01_" + str(bank))
                    for k in range(KCH):
                        P.do("tensor", lambda tn, c=c, k=k, bank=bank: tn.matmul(
                            ps01[:, bank * 512:bank * 512 + 256],
                            RWqkv[:, k * 3 * E + c * 128: k * 3 * E + (c + 1) * 128],
                            xT[:, k * 256:(k + 1) * 256],
                            start=(k == 0), stop=(k == KCH - 1)),
                            inc="pe" if k == KCH - 1 else None,
                            waits=w if k == 0 else ())
                    mil["qk_mm", t, c] = P.c["pe"]
                    P.do("scalar", lambda sC, c=c, bank=bank: sC.activation(
                        out=qkT[:, c * 256:(c + 1) * 256],
                        in_=ps01[:, bank * 512:bank * 512 + 256],
                        func=AF.Identity, bias=bqkvT[:, c:c + 1],
                        scale=QSCALE if c < KCH else 1.0),
                        inc="sc",
                        waits=[("pe", mil["qk_mm", t, c]), ("ve", mil["bqscale"]),
                               ("sb1", mil.get("dump_qkT", 0))])
                    mil["qk_evac", t, c] = P.c["sc"]
                mil["qk_done", t] = P.c["sc"]
                P.rel("ps01_0", "sc")
                P.rel("ps01_1", "sc")

            def tile_front(t):
                s, p = t // 2, t % 2
                x1off = p * 2048

                if p == 1 and s > 0:
                    for g in (0, 1):
                        emit_wgu_set(s, g)
                    for k in range(4):
                        emit_wd(s, k)
                for st in range(2):
                    w = P.acq("ps23") + [("sc", mil["bcast"]), ("ve", mil["xt_evac", t, 1])]
                    for half in range(2):
                        for k in range(KCH):
                            P.do("tensor", lambda tn, st=st, half=half, k=k: tn.matmul(
                                ps23[:, half * 512:(half + 1) * 512],
                                xT[:, k * 256 + st * 128: k * 256 + st * 128 + 128],
                                RWqkv[:, k * 3 * E + 2 * E + half * 512: k * 3 * E + 2 * E + (half + 1) * 512],
                                start=(k == 0), stop=False),
                                waits=w if (half == 0 and k == 0) else ())
                        P.do("tensor", lambda tn, half=half: tn.matmul(
                            ps23[:, half * 512:(half + 1) * 512],
                            onesb[0:1, :], brow[0:1, E + half * 512:E + (half + 1) * 512],
                            start=False, stop=True),
                            inc="pe" if half == 1 else None)
                    mil["v_mm", t, st] = P.c["pe"]
                    P.do("vector", lambda v, st=st: v.tensor_copy(
                        out=vtok[:, st * 1024:(st + 1) * 1024], in_=ps23[:]),
                        inc="ve", waits=[("pe", mil["v_mm", t, st]),
                                         ("sb2", mil.get("dump_vtok", 0))])
                    mil["v_evac", t, st] = P.c["ve"]
                    P.rel("ps23", "ve")

                if dbg and t == DBG_T:
                    P.do("gpsimd", lambda q: q.dma_start(out=dbg_t["d_xT"], in_=xT[:]),
                         inc="sb0", amt=16, waits=[("ve", mil["xt_evac", DBG_T, 1])])
                    mil["dump_xT"] = 16
                    P.do("gpsimd", lambda q: q.dma_start(out=dbg_t["d_qkT"], in_=qkT[:]),
                         inc="sb1", amt=16, waits=[("sc", mil["qk_done", DBG_T])])
                    mil["dump_qkT"] = 16
                    P.do("gpsimd", lambda q: q.dma_start(out=dbg_t["d_vtok"], in_=vtok[:]),
                         inc="sb2", amt=16, waits=[("ve", mil["v_evac", DBG_T, 1])])
                    mil["dump_vtok"] = 16

                if STAGE <= 3:
                    return
                def attn_A(st, hh):
                    for hh in (hh,):
                        col = st * 2 + hh
                        pst, pn = (ps45, "ps45") if hh == 0 else (ps67, "ps67")
                        eo = hh * 1024
                        w = P.acq(pn) + [("sc", mil["qk_done", t]), ("sp1", mil["consts"])]
                        for i8, h8 in enumerate((0, 2, 4, 6, 1, 3, 5, 7)):
                            h = hh * 8 + h8
                            slot = (h8 % 2) * 4 + h8 // 2
                            qpo = (h % 2) * 64
                            qc, kc = h // 2, KCH + h // 2
                            P.do("tensor", lambda tn, st=st, slot=slot, qpo=qpo, qc=qc, kc=kc, pst=pst: tn.matmul(
                                pst[:, slot * 128:(slot + 1) * 128],
                                qkT[qpo:qpo + 64, qc * 256 + st * 128: qc * 256 + st * 128 + 128],
                                qkT[qpo:qpo + 64, kc * 256 + st * 128: kc * 256 + st * 128 + 128],
                                start=True, stop=False),
                                waits=w if i8 == 0 else ())
                            P.do("tensor", lambda tn, slot=slot, qpo=qpo, pst=pst: tn.matmul(
                                pst[:, slot * 128:(slot + 1) * 128],
                                augcS[qpo:qpo + 4, 0:128],
                                augcS[qpo:qpo + 4, 128:256],
                                start=False, stop=True),
                                inc="pe" if i8 == 7 else None)
                        mil["scores", t, st, hh] = P.c["pe"]
                        P.do("scalar", lambda sC, pst=pst, eo=eo: sC.activation(
                            out=expb[:, eo:eo + 1024], in_=pst[:], func=AF.Exp),
                            inc="sc",
                            waits=[("pe", mil["scores", t, st, hh]),
                                   ("pe", mil.get(("last_trans", hh), 0))])
                        mil["exp", t, st, hh] = P.c["sc"]
                        mil["last_exp", hh] = P.c["sc"]
                        P.rel(pn, "sc")
                        if dbg and t == DBG_T and st == 0 and hh == 0:
                            P.do("gpsimd", lambda q: q.dma_start(out=dbg_t["d_expb"], in_=expb[:, 0:1024]),
                                 inc="sb10", amt=16, waits=[("sc", mil["exp", DBG_T, 0, 0])])
                            mil["dump_expb"] = 16
                        P.do("vector", lambda v, col=col, eo=eo: v.reduce_sum(
                            stats[:, SUMS + col * 8:SUMS + col * 8 + 8].rearrange("p (g o) -> p g o", g=8),
                            expb[:, eo:eo + 1024].rearrange("p (g m) -> p g m", g=8), axis=AX.X),
                            inc="ve", waits=[("sc", mil["exp", t, st, hh])])
                        mil["sums", t, st, hh] = P.c["ve"]
                        P.do("scalar", lambda sC, col=col, hh=hh: sC.activation(
                            out=stats[:, RCPC + hh * 8:RCPC + hh * 8 + 8],
                            in_=stats[:, SUMS + col * 8:SUMS + col * 8 + 8], func=AF.Ln),
                            inc="sc", waits=[("ve", mil["sums", t, st, hh])])
                        mil["rln", t, st, hh] = P.c["sc"]
                        P.do("scalar", lambda sC, hh=hh: sC.activation(
                            out=stats[:, RCPC + hh * 8:RCPC + hh * 8 + 8],
                            in_=stats[:, RCPC + hh * 8:RCPC + hh * 8 + 8],
                            func=AF.Exp, scale=-1.0), inc="sc",
                            waits=[("sc!", mil["rln", t, st, hh])])
                        mil["recip", t, st, hh] = P.c["sc"]
                        for h8 in range(8):
                            P.do("vector", lambda v, h8=h8, hh=hh, eo=eo: v.tensor_scalar_mul(
                                diag[:, eo + h8 * 128:eo + (h8 + 1) * 128], ident[:],
                                stats[:, RCPC + hh * 8 + h8:RCPC + hh * 8 + h8 + 1]),
                                inc="ve" if h8 == 7 else None,
                                waits=([("pe", mil.get(("last_trans", hh), 0)),
                                        ("sc", mil["recip", t, st, hh]),
                                        ("sb13", mil.get("dump_stats", 0))] if h8 == 0 else ()))
                        mil["diag", t, st, hh] = P.c["ve"]
                        if dbg and t == DBG_T and st == 0 and hh == 0:
                            P.do("gpsimd", lambda q: q.dma_start(out=dbg_t["d_diag"], in_=diag[:, 0:1024]),
                                 inc="sb12", amt=16, waits=[("ve", mil["diag", DBG_T, 0, 0])])
                            mil["dump_diag"] = 16
                            P.do("gpsimd", lambda q: q.dma_start(out=dbg_t["d_stats"], in_=stats[:]),
                                 inc="sb13", amt=16, waits=[("ve", mil["diag", DBG_T, 0, 0])])
                            mil["dump_stats"] = 16

                def attn_T(st, hh):
                    for hh in (hh,):
                        pst, pn = (ps45, "ps45") if hh == 0 else (ps67, "ps67")
                        eo = hh * 1024
                        w = P.acq(pn)
                        for h8 in range(8):
                            P.do("tensor", lambda tn, h8=h8, pst=pst, eo=eo: tn.matmul(
                                pst[:, h8 * 128:(h8 + 1) * 128],
                                expb[:, eo + h8 * 128:eo + (h8 + 1) * 128],
                                diag[:, eo + h8 * 128:eo + (h8 + 1) * 128],
                                start=True, stop=True),
                                inc="pe" if h8 == 7 else None,
                                waits=(w + [("ve", mil["diag", t, st, hh])]) if h8 == 0 else ())
                        mil["trans", t, st, hh] = P.c["pe"]
                        mil["last_trans", hh] = P.c["pe"]
                        P.do("vector", lambda v, pst=pst, eo=eo: v.tensor_copy(
                            out=attnT[:, eo:eo + 1024], in_=pst[:]),
                             inc="ve", waits=[("pe", mil["trans", t, st, hh]),
                                              ("sb11", mil.get("dump_attnT", 0))])
                        mil["attnT", t, st, hh] = P.c["ve"]
                        P.rel(pn, "ve")
                        if dbg and t == DBG_T and st == 0 and hh == 0:
                            P.do("gpsimd", lambda q: q.dma_start(out=dbg_t["d_attnT"], in_=attnT[:, 0:1024]),
                                 inc="sb11", amt=16, waits=[("ve", mil["attnT", DBG_T, 0, 0])])
                            mil["dump_attnT"] = 16
                def attn_O(st, hh):
                    for hh in (hh,):
                        eo = hh * 1024
                        bank = hh
                        w = P.acq("ps01_" + str(bank))
                        for i8 in range(8):
                            h8 = (i8 % 4) * 2 + i8 // 4
                            h = hh * 8 + h8
                            po = (h % 2) * 64
                            ci = (h8 // 2)
                            P.do("tensor", lambda tn, st=st, h=h, i8=i8, po=po, ci=ci, bank=bank, eo=eo: tn.matmul(
                                ps01[po:po + 64, bank * 512 + ci * 128: bank * 512 + (ci + 1) * 128],
                                vtok[:, st * 1024 + h * 64: st * 1024 + (h + 1) * 64],
                                attnT[:, eo + i8 * 128:eo + (i8 + 1) * 128],
                                start=True, stop=True),
                                inc="pe" if i8 == 7 else None,
                                waits=(w + [("ve", mil["attnT", t, st, hh]), ("ve", mil["v_evac", t, st]), ("sc", mil["qk_done", t])]) if i8 == 0 else ())
                        mil["out", t, st, hh] = P.c["pe"]
                        P.do("vector", lambda v, st=st, hh=hh, bank=bank: v.tensor_copy(
                            out=aoT[:, hh * 4 * 256:(hh + 1) * 4 * 256].rearrange(
                                "p (c j) -> p c j", j=256)[:, :, st * 128:st * 128 + 128],
                            in_=ps01[:, bank * 512:(bank + 1) * 512].rearrange("p (c j) -> p c j", j=128)),
                            inc="ve", waits=[("pe", mil["out", t, st, hh])])
                        mil["aoT", t, st, hh] = P.c["ve"]
                        P.rel("ps01_" + str(bank), "ve")

                def wout_st(st):
                    w = P.acq("ps23") + [("ve", mil["aoT", t, st, 1]), ("sc", mil["bcast"]), ("sp3", mil["rwout"])]
                    for half in range(2):
                        for k in range(KCH):
                            P.do("tensor", lambda tn, st=st, half=half, k=k: tn.matmul(
                                ps23[:, half * 512:(half + 1) * 512],
                                aoT[:, k * 256 + st * 128: k * 256 + st * 128 + 128],
                                RWout[:, k * E + half * 512: k * E + (half + 1) * 512],
                                start=(k == 0), stop=False),
                                waits=w if (half == 0 and k == 0) else ())
                        P.do("tensor", lambda tn, half=half: tn.matmul(
                            ps23[:, half * 512:(half + 1) * 512],
                            onesb[0:1, :], brow[0:1, half * 512:(half + 1) * 512],
                            start=False, stop=True),
                            inc="pe" if half == 1 else None)
                    mil["wout", t, st] = P.c["pe"]
                    P.do("vector", lambda v, st=st: v.tensor_tensor(
                        x1[:, x1off + st * 1024: x1off + (st + 1) * 1024],
                        ps23[:],
                        xf[:, (t % 2) * 2048 + st * 1024:(t % 2) * 2048 + (st + 1) * 1024],
                        ALU.add),
                        inc="ve",
                        waits=[("pe", mil["wout", t, st]), ("sx" + str(t % 2), mil["xload", t]),
                               ("sy" + str(t % 2), mil.get(("store", t - 2), 0))])
                    mil["resid1", t, st] = P.c["ve"]
                    P.rel("ps23", "ve")

                attn_A(0, 0)
                attn_A(0, 1)
                if t + 1 < n_tiles:
                    front_x(t + 1, wait_scores=mil["scores", t, 0, 1])
                attn_T(0, 0)
                attn_T(0, 1)
                attn_O(0, 0)
                attn_A(1, 0)
                attn_O(0, 1)
                attn_A(1, 1)
                wout_st(0)
                attn_T(1, 0)
                attn_T(1, 1)
                attn_O(1, 0)
                attn_O(1, 1)
                wout_st(1)

                if t + 2 < n_tiles:
                    P.do("gpsimd", lambda s, t=t: s.dma_start(
                        out=xf[:, (t % 2) * 2048:(t % 2) * 2048 + 2048].rearrange("p (st j) -> p st j", st=2),
                        in_=xsrc(t + 2)), inc="sx" + str(t % 2), amt=16,
                        waits=[("ve", mil["resid1", t, 1])])
                    mil["xload", t + 2] = P.c["sx" + str(t % 2)]

                if EARLY_A and t + 1 < n_tiles:
                    front_x(t + 1)
                    front_q(t + 1)

                if STAGE <= 7:
                    return
                if dbg and t == DBG_T:
                    P.do("gpsimd", lambda q: q.dma_start(out=dbg_t["d_aoT"], in_=aoT[:]),
                         inc="sb5", amt=16, waits=[("ve", mil["aoT", DBG_T, 1, 1])])
                    mil["dump_aoT"] = 16
                    P.do("gpsimd", lambda q: q.dma_start(out=dbg_t["d_x1a"], in_=x1[:, (DBG_T % 2) * 2048:(DBG_T % 2) * 2048 + 2048]),
                         inc="sb6", amt=16, waits=[("ve", mil["resid1", DBG_T, 1])])
                    mil["dump_x1a"] = 16

                if STAGE <= 7:
                    return
                S1, Q1, M1, MS1, V1, R1 = 36, 38, 40, 42, 44, 46
                P.do("vector", lambda v: v.reduce_sum(
                    stats[:, S1:S1 + 2].rearrange("p (g o) -> p g o", g=2),
                    x1[:, x1off:x1off + 2048].rearrange("p (g j) -> p g j", g=2),
                    axis=AX.X),
                    inc="ve", waits=[("ve!", mil["resid1", t, 1])])
                mil["lnsum", t] = P.c["ve"]
                for st in range(2):
                    pssq = ps45 if st == 0 else ps67
                    psname = "ps45" if st == 0 else "ps67"
                    xsl = x1[:, x1off + st * 1024: x1off + (st + 1) * 1024]
                    w = P.acq(psname)
                    P.do("scalar", lambda sC, xsl=xsl, pssq=pssq, st=st: sC.activation(
                        out=pssq[:], in_=xsl, func=AF.Square,
                        accum_out=stats[:, Q1 + st:Q1 + st + 1]),
                        inc="sc",
                        waits=w + [("ve", mil["resid1", t, st])])
                    mil["square", t, st] = P.c["sc"]
                    P.rel(psname, "sc")
                P.do("vector", lambda v: v.tensor_scalar_mul(
                    stats[:, M1:M1 + 2], stats[:, S1:S1 + 2], 1.0 / E),
                    inc="ve", waits=[("ve!", mil["lnsum", t])])
                mil["mean1", t] = P.c["ve"]
                P.do("vector", lambda v: v.tensor_tensor(
                    stats[:, MS1:MS1 + 2], stats[:, M1:M1 + 2],
                    stats[:, M1:M1 + 2], ALU.mult),
                    inc="ve", waits=[("ve!", mil["mean1", t])])
                mil["msq1", t] = P.c["ve"]
                P.do("vector", lambda v: v.scalar_tensor_tensor(
                    stats[:, V1:V1 + 2], stats[:, Q1:Q1 + 2], 1.0 / E,
                    stats[:, MS1:MS1 + 2], ALU.mult, ALU.subtract),
                    inc="ve", waits=[("sc", mil["square", t, 1]), ("ve!", mil["msq1", t])])
                mil["var1", t] = P.c["ve"]
                P.do("scalar", lambda sC: sC.activation(
                    out=stats[:, R1:R1 + 2], in_=stats[:, V1:V1 + 2], func=AF.Ln,
                    bias=stats[:, EPSC:EPSC + 1]),
                    inc="sc", waits=[("ve", mil["var1", t])])
                mil["lnln1", t] = P.c["sc"]
                P.do("scalar", lambda sC: sC.activation(
                    out=stats[:, R1:R1 + 2], in_=stats[:, R1:R1 + 2],
                    func=AF.Exp, scale=-0.5),
                    inc="sc", waits=[("sc!", mil["lnln1", t])])
                mil["rstd1", t] = P.c["sc"]
                for st in range(2):
                    xsl = x1[:, x1off + st * 1024: x1off + (st + 1) * 1024]
                    P.do("vector", lambda v, xsl=xsl, st=st: v.tensor_scalar(
                        xsl, xsl, stats[:, M1 + st:M1 + st + 1], stats[:, R1 + st:R1 + st + 1],
                        ALU.subtract, ALU.mult),
                        inc="ve", waits=([("sc", mil["rstd1", t]),
                                          ("sb6", mil.get("dump_x1a", 0))] if st == 0 else ()))
                    mil["norm1", t, st] = P.c["ve"]

                if dbg and t == DBG_T:
                    P.do("gpsimd", lambda q: q.dma_start(out=dbg_t["d_x1"], in_=x1[:, (DBG_T % 2) * 2048:(DBG_T % 2) * 2048 + 2048]),
                         inc="sb7", amt=16, waits=[("ve", mil["norm1", DBG_T, 1])])
                    mil["dump_x1"] = P.c["sb7"]

                for st in range(2):
                    xsl = x1[:, x1off + st * 1024: x1off + (st + 1) * 1024]
                    P.do("vector", lambda v, xsl=xsl, st=st: v.tensor_copy(
                        out=aoT[:, st * 1024:(st + 1) * 1024], in_=xsl),
                        inc="ve",
                        waits=[("ve!", mil["norm1", t, st]), ("pe", mil["wout", t, 1]),
                               ("sb5", mil.get("dump_aoT", 0))])
                    mil["x1b", t, st] = P.c["ve"]
                    w = P.acq("ps23")
                    for c in range(KCH):
                        P.do("tensor", lambda tn, st=st, c=c: tn.matmul(
                            ps23[:, c * 128:(c + 1) * 128],
                            aoT[:, st * 1024 + c * 128: st * 1024 + (c + 1) * 128],
                            ident[:], start=True, stop=True),
                            inc="pe" if c == KCH - 1 else None,
                            waits=(w + [("ve", mil["x1b", t, st])]) if c == 0 else ())
                    mil["x1t_mm", t, st] = P.c["pe"]
                    l = p * 2 + st
                    P.do("vector", lambda v, l=l: v.tensor_copy(
                        out=x1T[:].rearrange("p (k j) -> p k j", j=512)[:, :, l * 128:l * 128 + 128],
                        in_=ps23[:].rearrange("p (k j) -> p k j", j=128)),
                        inc="ve",
                        waits=[("pe", mil["x1t_mm", t, st]),
                               ("pe", mil.get(("gu_done", s - 1), 0) if (t % 2 == 0 and st == 0) else 0)])
                    mil["x1t_evac", t, st] = P.c["ve"]
                    P.rel("ps23", "ve")

            def emit_wgu_set(s, g):
                gi = s * NGRP + g
                buf = gi % 2
                boff = buf * (2 * KCH * GC * 128)
                w = [("pe", mil.get(("gu_grp", gi - 2), 0)),
                     ("sps0", mil.get("gu_stores0", 0)), ("sps1", mil.get("gu_stores1", 0)),
                     ("sps2", mil.get("gu_stores2", 0)), ("sps3", mil.get("gu_stores3", 0)),
                     ("sb4", mil.get("dump_wgu", 0))]
                par = str(gi % 2)
                for wi, natT in ((0, wgT_s), (1, wuT_s)):
                    sem = ("swp" if wi == 0 else "svp") + par
                    P.do("sync", lambda q, natT=natT, g=g, wi=wi, boff=boff: q.dma_start(
                        out=wgu[:, boff + wi * (KCH * GC * 128):
                                boff + (wi + 1) * (KCH * GC * 128)].rearrange(
                                    "p (k j) -> p k j", j=GC * 128),
                        in_=natT[0:E, g * GC * 128:(g + 1) * GC * 128].rearrange(
                            "(k p) j -> p k j", p=128)),
                        inc=sem, amt=16, waits=w if wi == 0 else ())
                mil["wgu_set_sw", gi] = ("swp" + par, P.c["swp" + par])
                mil["wgu_set_sv", gi] = ("svp" + par, P.c["svp" + par])

            def emit_wd(s, k):
                w = [("pe", mil.get(("down_m", s, k - 4), mil.get(("down_done", s - 1), 0))),
                     ("sps0", mil.get("prep_stores0", 0)), ("sps1", mil.get("prep_stores1", 0)),
                     ("sb8", mil.get("dump_wdb", 0))]
                par = str(k % 4)
                P.do("scalar", lambda q, k=k: q.dma_start(
                    out=wdb[:, (k % 4) * E:(k % 4 + 1) * E],
                    in_=wdT_s[k * 128:(k + 1) * 128, 0:E]),
                    inc="sd" + par, amt=16, waits=w)
                mil["wd_dma", s, k] = ("sd" + par, P.c["sd" + par])

            def ffn(s):
                if dbg and s == DBG_S:
                    P.do("gpsimd", lambda q: q.dma_start(out=dbg_t["d_x1T"], in_=x1T[:]),
                         inc="sb3", amt=16, waits=[("ve", mil["x1t_evac", 2 * DBG_S + 1, 1])])
                    mil["dump_x1T"] = 16
                    P.do("gpsimd", lambda q: q.dma_start(out=dbg_t["d_wgu"], in_=wgu[:]),
                         inc="sb4", amt=16,
                         waits=[mil["wgu_set_sw", DBG_S * NGRP], mil["wgu_set_sv", DBG_S * NGRP],
                                mil["wgu_set_sw", DBG_S * NGRP + 1], mil["wgu_set_sv", DBG_S * NGRP + 1]])
                    mil["dump_wgu"] = 16

                for g in range(NGRP):
                    gi = s * NGRP + g
                    buf = gi % 2
                    boff = buf * (2 * KCH * GC * 128)
                    for mi_ in range(GC):
                        m = g * GC + mi_
                        pair = ps45 if m % 2 == 0 else ps67
                        pname = "ps45" if m % 2 == 0 else "ps67"
                        w = P.acq(pname) + [
                            mil["wgu_set_sw", gi], mil["wgu_set_sv", gi],
                            ("ve", mil["x1t_evac", 2 * s + 1, 1]),
                            ("ve", mil.get(("hmul", s, m - 2), 0)),
                            ("sc", mil.get(("silu", s, m - 2), 0))]
                        for wi in range(2):
                            for k in range(KCH):
                                P.do("tensor", lambda tn, wi=wi, k=k, mi_=mi_, boff=boff, pair=pair: tn.matmul(
                                    pair[:, wi * 512:(wi + 1) * 512],
                                    wgu[:, boff + wi * (KCH * GC * 128) + k * (GC * 128) + mi_ * 128:
                                        boff + wi * (KCH * GC * 128) + k * (GC * 128) + (mi_ + 1) * 128],
                                    x1T[:, k * 512:(k + 1) * 512],
                                    start=(k == 0), stop=(k == KCH - 1)),
                                    inc="pe" if k == KCH - 1 else None,
                                    waits=w if (wi == 0 and k == 0) else ())
                            if wi == 0:
                                mil["gate", s, m] = P.c["pe"]
                            else:
                                mil["up", s, m] = P.c["pe"]
                        P.do("scalar", lambda sC, m=m, pair=pair: sC.activation(
                            out=silu2[:, (m % 2) * 512:(m % 2 + 1) * 512],
                            in_=pair[:, 0:512], func=AF.Silu, scale=S127),
                            inc="sc",
                            waits=[("pe", mil["gate", s, m]), ("ve", mil.get(("hmul", s, m - 2), 0))])
                        mil["silu", s, m] = P.c["sc"]
                        P.do("vector", lambda v, m=m, pair=pair: v.tensor_tensor(
                            hT[:, m * 512:(m + 1) * 512],
                            silu2[:, (m % 2) * 512:(m % 2 + 1) * 512],
                            pair[:, 512:1024], ALU.mult),
                            inc="ve",
                            waits=[("sc", mil["silu", s, m]), ("pe", mil["up", s, m]),
                                   ("pe", mil.get(("down_done", s - 1), 0))])
                        mil["hmul", s, m] = P.c["ve"]
                    mil["gu_grp", gi] = P.c["pe"]
                    if g + 2 < NGRP:
                        emit_wgu_set(s, g + 2)
                mil["gu_done", s] = P.c["pe"]

                if dbg and s == DBG_S:
                    P.do("gpsimd", lambda q: q.dma_start(out=dbg_t["d_hT"], in_=hT[:]),
                         inc="sb7", amt=16, waits=[("ve", mil["hmul", DBG_S, MCH - 1])])
                    mil["dump_hT"] = P.c["sb7"]
                    P.do("gpsimd", lambda q: q.dma_start(out=dbg_t["d_wdb"], in_=wdb[:]),
                         inc="sb8", amt=16,
                         waits=([("spt0", mil["wdt", 2]), ("spt1", mil["wdt", 3])]
                                if s == 0 else [("sd0", 16), ("sd1", 16)]))
                    mil["dump_wdb"] = 16

                if STAGE <= 9:
                    return
                psl = [ps01, ps23, ps45, ps67]
                for m in range(MCH):
                    wslot = (m // 2) % 2
                    coff = (m % 2) * E
                    w = [(("spt" + str(m % 2), mil["wdt", m]) if s == 0
                          else mil["wd_dma", s, m]), ("ve", mil["hmul", s, m])]
                    if m == 0:
                        w += [("ve", mil["hmul", s, MCH - 1]), ("sb8", mil.get("dump_wdb", 0))]
                        w += P.acq("ps01_0") + P.acq("ps01_1") + P.acq("ps23") + P.acq("ps45") + P.acq("ps67")
                        w += [("ve", mil["aoT", 2 * s + 1, 1, 1]), ("ve", mil["x1t_evac", 2 * s + 1, 1])]
                    for l in range(4):
                        for half in range(2):
                            P.do("tensor", lambda tn, m=m, l=l, half=half, wslot=wslot, coff=coff: tn.matmul(
                                psl[l][:, half * 512:(half + 1) * 512],
                                hT[:, m * 512 + l * 128: m * 512 + l * 128 + 128],
                                wdb[:, wslot * 2 * E + coff + half * 512: wslot * 2 * E + coff + (half + 1) * 512],
                                start=(m == 0), stop=(m == MCH - 1)),
                                inc="pe" if (l == 3 and half == 1) else None,
                                waits=w if (l == 0 and half == 0) else ())
                    mil["down_m", s, m] = P.c["pe"]
                    if m + 4 < MCH:
                        if s == 0:
                            emit_wdt(m + 4, extra=[("pe", mil["down_m", 0, m])])
                            emit_wdst(m + 4)
                        else:
                            emit_wd(s, m + 4)
                mil["down_done", s] = P.c["pe"]
                if s == 0:
                    mil["prep_stores0"] = P.c["sps0"]
                    mil["prep_stores1"] = P.c["sps1"]

                for l in range(4):
                    xsl = x1[:, l * 1024:(l + 1) * 1024]
                    P.do("vector", lambda v, xsl=xsl, l=l: v.scalar_tensor_tensor(
                        xsl, psl[l][:], S127SQ, xsl, ALU.mult, ALU.add),
                        inc="ve", waits=([("pe", mil["down_done", s]),
                                          ("sb7", mil.get("dump_hT", 0))] if l == 0 else ()))
                    mil["resid2", s, l] = P.c["ve"]
                    if l == 0:
                        P.rel("ps01_0", "ve")
                        P.rel("ps01_1", "ve")
                    if dbg and s == DBG_S and l == 1:
                        P.do("gpsimd", lambda q: q.dma_start(out=dbg_t["d_t2"], in_=x1[:, 2048:4096]),
                             inc="sb9", amt=16, waits=[("ve", mil["resid2", DBG_S, 1])])
                        mil["dump_t2"] = 16
                for l in range(4):
                    xsl = x1[:, l * 1024:(l + 1) * 1024]
                    dump = [ps45, ps23, ps45, ps67][l]
                    P.do("scalar", lambda sC, xsl=xsl, dump=dump, l=l: sC.activation(
                        out=dump[:], in_=xsl, func=AF.Square,
                        accum_out=stats[:, Q2c + l:Q2c + l + 1]),
                        inc="sc", waits=[("ve", mil["resid2", s, max(l, 2) if l != 1 else 1])]
                        + ([("sc!", mil["square2", s, 0])] if l == 2 else []))
                    mil["square2", s, l] = P.c["sc"]
                    if l >= 1:
                        P.rel(["ps01_0", "ps23", "ps45", "ps67"][l], "sc")
                P.do("vector", lambda v: v.reduce_sum(
                    stats[:, S2c:S2c + 4].rearrange("p (g o) -> p g o", g=4),
                    x1[:].rearrange("p (g j) -> p g j", g=4), axis=AX.X),
                    inc="ve", waits=[("ve!", mil["resid2", s, 3])])
                mil["sum2", s] = P.c["ve"]
                P.do("vector", lambda v: v.tensor_scalar_mul(
                    stats[:, M2c:M2c + 4], stats[:, S2c:S2c + 4], 1.0 / E),
                    inc="ve", waits=[("ve!", mil["sum2", s])])
                mil["mean2", s] = P.c["ve"]
                P.do("vector", lambda v: v.tensor_tensor(
                    stats[:, MS2c:MS2c + 4], stats[:, M2c:M2c + 4],
                    stats[:, M2c:M2c + 4], ALU.mult),
                    inc="ve", waits=[("ve!", mil["mean2", s])])
                mil["msq2", s] = P.c["ve"]
                P.do("vector", lambda v: v.scalar_tensor_tensor(
                    stats[:, V2c:V2c + 4], stats[:, Q2c:Q2c + 4], 1.0 / E,
                    stats[:, MS2c:MS2c + 4], ALU.mult, ALU.subtract),
                    inc="ve", waits=[("sc", mil["square2", s, 3]), ("ve!", mil["msq2", s])])
                mil["var2", s] = P.c["ve"]
                P.do("scalar", lambda sC: sC.activation(
                    out=stats[:, R2c:R2c + 4], in_=stats[:, V2c:V2c + 4], func=AF.Ln,
                    bias=stats[:, EPSC:EPSC + 1]),
                    inc="sc", waits=[("ve", mil["var2", s])])
                mil["lnln2", s] = P.c["sc"]
                P.do("scalar", lambda sC: sC.activation(
                    out=stats[:, R2c:R2c + 4], in_=stats[:, R2c:R2c + 4],
                    func=AF.Exp, scale=-0.5),
                    inc="sc", waits=[("sc!", mil["lnln2", s])])
                mil["rstd2", s] = P.c["sc"]
                for l in range(4):
                    xsl = x1[:, l * 1024:(l + 1) * 1024]
                    P.do("vector", lambda v, xsl=xsl, l=l: v.tensor_scalar(
                        xsl, xsl, stats[:, M2c + l:M2c + l + 1], stats[:, R2c + l:R2c + l + 1],
                        ALU.subtract, ALU.mult),
                        inc="ve", waits=([("sc", mil["rstd2", s]),
                                          ("sb9", mil.get("dump_t2", 0))] if l == 0 else ()))
                    mil["norm2", s, l] = P.c["ve"]

                for p in range(2):
                    t = 2 * s + p
                    P.do("gpsimd", lambda sq, t=t, p=p: sq.dma_start(
                        out=y_out[t * TTOK:(t + 1) * TTOK, :].rearrange("(st pp) j -> pp st j", pp=128),
                        in_=x1[:, p * 2048:(p + 1) * 2048].rearrange("q (st j) -> q st j", st=2)),
                        inc="sy" + str(t % 2), amt=16,
                        waits=[("ve", mil["norm2", s, 2 * p + 1]), ("ve", mil["norm2", s, 2 * p])])
                    mil["store", t] = P.c["sy" + str(t % 2)]

            front_x(0)
            front_q(0)
            emit_prep_casts()
            for s in range(n_tiles // 2):
                tile_front(2 * s)
                if STAGE <= 8:
                    break
                if not EARLY_A:
                    front_q(2 * s + 1)
                if s == 0:
                    emit_prep_casts2()
                tile_front(2 * s + 1)
                if not EARLY_A and 2 * s + 2 < n_tiles:
                    front_q(2 * s + 2)
                ffn(s)
                if STAGE <= 10:
                    break

            for nm in SEM_PROD:
                if P.c[nm] > 0 and SEM_PROD[nm] != "sync":
                    P.do("sync", lambda sq, nm=nm, v=P.c[nm]: sq.wait_ge(sems[nm], v))
            for nm in SEM_PROD:
                if P.c[nm] > 0 and SEM_PROD[nm] == "sync":
                    P.do("sync", lambda sq, nm=nm, v=P.c[nm]: sq.wait_ge(sems[nm], v))

        @block.tensor
        def _(tn):
            body(Prog("tensor", {"tensor": tn}, sems))

        @block.vector
        def _(v):
            body(Prog("vector", {"vector": v}, sems))

        @block.scalar
        def _(sC):
            body(Prog("scalar", {"scalar": sC}, sems))

        @block.sync
        def _(sq):
            body(Prog("sync", {"sync": sq}, sems))

        @block.gpsimd
        def _(g):
            body(Prog("gpsimd", {"gpsimd": g}, sems))

    return nc


def make_host_inputs(inputs, shard):
    b_qkv = inputs["L0_attn_b_qkv"]
    mask = np.zeros((128, 128), dtype=np.float32)
    mask[:64, 64:] = -1e9
    mask[64:, :64] = -1e9
    BIG = 32768.0
    SHIFT = 20.0
    bA = (np.arange(128) < 64).astype(np.float32)
    bB = 1.0 - bA
    augc = np.zeros((128, 256), dtype=np.float32)
    for base in (0, 64):
        augc[base + 0, 0:128] = 1.0
        augc[base + 1, 0:128] = bA
        augc[base + 2, 0:128] = bB
        augc[base + 3, 0:128] = 1.0
        augc[base + 0, 128:256] = -BIG
        augc[base + 1, 128:256] = BIG * bA
        augc[base + 2, 128:256] = BIG * bB
        augc[base + 3, 128:256] = -SHIFT
    return dict(
        x=np.ascontiguousarray(shard),
        wqkv=inputs["L0_attn_W_qkv"],
        wout=inputs["L0_attn_W_out"],
        wg=inputs["L0_ffn_w_gate"],
        wu=inputs["L0_ffn_w_up"],
        wd=inputs["L0_ffn_w_down"],
        bqkvT=np.ascontiguousarray(b_qkv.reshape(3 * KCH, 128).T),
        bout=np.ascontiguousarray(inputs["L0_attn_b_out"][None, :]),
        bv=np.ascontiguousarray(b_qkv[None, 2 * E:3 * E]),
        ident=np.eye(128, dtype=BF),
        ones=np.ones((1, 128), dtype=BF),
        mask=mask,
        augc=augc.astype(BF),
    )


def kernel(**inputs):
    x = np.asarray(inputs["L0_attn_x"])
    n = 8
    inputs = {k: np.ascontiguousarray(np.asarray(v)) for k, v in inputs.items()}
    nc = build_nc(NT)
    in_maps = [make_host_inputs(inputs, x[i]) for i in range(n)]
    res = run_bass_kernel_spmd(nc, in_maps, core_ids=list(range(n)))
    out = np.stack([np.asarray(r["y"], dtype=np.float32) for r in res.results], axis=0)
    return out
